# version 17
# baseline (speedup 1.0000x reference)
import math

import numpy as np
import ml_dtypes

import concourse.bass as bass
import concourse.mybir as mybir
from concourse.bass_utils import run_bass_kernel_spmd

F32 = mybir.dt.float32
BF16 = mybir.dt.bfloat16
I32 = mybir.dt.int32
AF = mybir.ActivationFunctionType
ALU = mybir.AluOpType

B, S, D, H, HD = 4, 4096, 1024, 16, 64
T = 2048
DC = 8
FFH = 3072
NH = FFH // 128
TG = 1024
NTG = T // TG
QT = 512
NQC = T // QT
KC = S // 128
NPAIR = H // 2
EPS = 1e-5
WS = 1.0 / 127.0
KB = 1024
NTT = T // 128
LAST_RES = None


def build_program(dbg=None):
    M = {}
    _emit(bass.Bass("TRN2", target_bir_lowering=False, debug=False), M, True, dbg)
    nc = bass.Bass("TRN2", target_bir_lowering=False, debug=False)
    _emit(nc, M, False, dbg)
    return nc


def _emit(nc, M, dry, dbg=None):

    dt = nc.dram_tensor
    x_in = dt("x_in", [T, D], F32, kind="ExternalInput").ap()
    wqL = dt("wqL", [DC, 128, D], F32, kind="ExternalInput").ap()
    wkL = dt("wkL", [DC, 128, D], F32, kind="ExternalInput").ap()
    wvL = dt("wvL", [DC, 128, D], F32, kind="ExternalInput").ap()
    woL = dt("woL", [DC, 128, D], F32, kind="ExternalInput").ap()
    biases = dt("biases", [4, D], F32, kind="ExternalInput").ap()
    wguL = dt("wguL", [NH, 128, 2 * D], I32, kind="ExternalInput").ap()
    wdL = dt("wdL", [NH, 128, D], I32, kind="ExternalInput").ap()
    caq = dt("caq", [128, 2 * T], F32, kind="ExternalInput").ap()
    cak = dt("cak", [128, 2 * T], F32, kind="ExternalInput").ap()
    ident = dt("ident", [128, 128], F32, kind="ExternalInput").ap()
    permAB = dt("permAB", [128, 256], BF16, kind="ExternalInput").ap()
    fconsts = dt("fconsts", [1, 16], F32, kind="ExternalInput").ap()

    y_out = dt("y_out", [T, D], F32, kind="ExternalOutput").ap()
    dbg_out = dt("dbg_out", [128, 32768], F32, kind="ExternalOutput").ap() if dbg else None
    dbg_out2 = dt("dbg_out2", [128, 32768], F32, kind="ExternalOutput").ap() if dbg else None

    x1sp = dt("x1sp", [128, DC * T], F32)
    x2sp = dt("x2sp", [128, DC * T], F32)
    v_snd = dt("v_snd", [128, 16384], BF16)
    v_rcv = dt("v_rcv", [256, 16384], BF16)
    k_snd = dt("k_snd", [128, 16384], BF16)
    k_rcv = dt("k_rcv", [256, 16384], BF16)
    rbd = dt("rbd", [2, QT], F32)

    arena = nc.alloc_sbuf_tensor("arena", [128, 52900], F32)
    A0 = nc.lookup_mls("arena").memorylocations[0].addr
    A0 = (A0 + 31) // 32 * 32

    def at(name, shape, dtyp, off):
        return nc.alloc_sbuf_tensor_at(name, shape, dtyp, offset=A0 + off)

    R0, R1, R2, R3 = 0, 64 * KB, 128 * KB, 192 * KB
    xs = at("xs", [128, DC * T], F32, R0)
    xln2 = at("xln2", [128, DC * T], BF16, R0)
    x2stg = at("x2stg", [128, DC * QT], F32, R0)
    nsq = at("nsq", [128, S], BF16, R0 + 16 * KB)
    attn_fm = at("attn_fm", [128, DC * QT], BF16, R0 + 16 * KB)
    pT = at("pT", [128, 2 * 2 * QT], BF16, R0 + 24 * KB)
    woS = at("woS", [128, 2 * D], BF16, R0 + 28 * KB)
    qrope = at("qrope", [128, DC * T], BF16, R0 + 32 * KB)
    kfull = at("kfull", [128, DC * S], BF16, R1)
    hbuf = at("hbuf", [128, NH * TG], BF16, R1)
    xlnf = at("xlnf", [128, DC * TG], BF16, R1 + 48 * KB)
    kstg = at("kstg", [128, DC * T], BF16, R1)
    wfull = at("wfull", [128, DC * D], BF16, R1 + 32 * KB)
    qraw = at("qraw", [128, T], BF16, R1 + 48 * KB)
    rtmp = at("rtmp", [128, T], F32, R1 + 52 * KB)
    vfA = at("vfA", [128, 31 * 1032 + 6 * 129], BF16, R2)
    vstg = at("vstg", [128, 16 * KB], BF16, R2)
    m2row = at("m2row", [1, T], F32, R2)
    rows = at("rows", [1, 2 * T], F32, R2 + 8 * KB)
    xbf = at("xbf", [128, 2 * T], BF16, R2 + 24 * KB)
    sqbf = at("sqbf", [128, 2 * T], BF16, R2 + 32 * KB)
    wgu_s = at("wgu_s", [128, 2 * 4 * D], BF16, R2 + 24 * KB)
    wd_s = at("wd_s", [128, 2 * 4 * D], BF16, R2 + 40 * KB)
    lntmp = at("lntmp", [128, 512], F32, R2 + 56 * KB)
    silu_g = at("silu_g", [128, 2 * TG], BF16, R2 + 58 * KB)
    stg = at("stg", [128, D], F32, R1 + 48 * KB)
    stg2 = at("stg2", [128, D], F32, R1 + 52 * KB)
    cab = at("cab", [128, 2 * T], F32, R2 + 32 * KB)
    cs_idf = at("cs_idf", [128, 128], F32, R3)
    cs_perm = at("cs_perm", [128, 256], BF16, R3 + 512)
    onesc_bf = at("onesc_bf", [128, 1], BF16, R3 + 1024)
    onesr_bf = at("onesr_bf", [1, T], BF16, R3 + 1056)
    onesf = at("onesf", [65, 128], F32, R3 + 1056 + 4 * KB)
    negs = at("negs", [1, 128], F32, R3 + 1568 + 4 * KB)
    biasb = at("biasb", [1, D], BF16, R3 + 2080 + 4 * KB)
    nrm = at("nrm", [1, 64], F32, R3 + 2080 + 6 * KB)
    negC = at("negC", [128, NPAIR], F32, R3 + 2336 + 6 * KB)
    dens = at("dens", [65, QT], F32, R3 + 2368 + 6 * KB)
    recips = at("recips", [65, QT], F32, R3 + 2368 + 8 * KB)
    rbs = at("rbs", [128, QT], F32, R3 + 2368 + 10 * KB)
    fc = at("fc", [1, 16], F32, R3 + 2368 + 12 * KB)
    vfB = at("vfB", [128, 2 * 129], BF16, R3 + 2368 + 12 * KB + 64)

    for i_, v_ in enumerate((EPS, WS)):
        nc.const_aps.register(F32, v_, fc[0:1, i_:i_ + 1])

    parena = nc.alloc_psum_tensor("parena", [128, 4096], F32)

    def bank(b, n=512):
        return parena[:, b * 512:b * 512 + n]

    def vf(kc, pr):
        if kc < 31:
            b = kc * 1032 + pr * 129
        elif pr < 6:
            b = 31 * 1032 + pr * 129
        else:
            return vfB[:, (pr - 6) * 129:(pr - 6) * 129 + 129]
        return vfA[:, b:b + 129]

    sems, cnt = {}, {}
    for s_ in ("ld", "lw", "sp", "cc", "pe", "ac", "dv", "gp", "ldc", "ldx",
               "lwg", "lwd", "lww", "lwb", "lwo", "cabq", "snd", "kin", "vin",
               "x2l", "bcd", "x2s", "sty", "sty2", "ldy", "vin2", "sp1", "dbg"):
        sems[s_] = nc.alloc_semaphore(s_)
        cnt[s_] = 0

    def inc(instr, name, by=1):
        instr.then_inc(sems[name], by)
        cnt[name] += by
        return cnt[name]

    def val(name):
        return cnt[name]

    def wait(eng, name, v):
        if v > 0:
            eng.wait_ge(sems[name], v)

    def chain(e, m):
        wait(e, "dv", inc(m, "dv"))

    def dgo(e, name, out, in_):
        wait(e, name, cnt[name])
        m = e.dma_start(out=out, in_=in_)
        m.then_inc(sems[name], 16)
        cnt[name] += 16
        return m

    def mark(key, name):
        if dry:
            M[key] = (name, cnt[name])
        else:
            assert M[key] == (name, cnt[name]), (key, M[key], (name, cnt[name]))

    def wt(eng, key):
        if key not in M:
            assert dry or dbg, "unknown mark %r in final pass" % (key,)
            return
        nm, v = M[key]
        wait(eng, nm, v)

    def xs_c(c, t0, t1):
        return xs[:, c * T + t0:c * T + t1]

    NQC_E = 1 if dbg in ("attn1", "attn1b", "attn2") else NQC
    NPAIR_E = 1 if dbg == "attn1" else NPAIR
    SKIP_OP = dbg in ("attn1", "attn1b")

    with nc.Block() as blk:

        def pro_sync(e):
            dgo(e, "ldc", cs_idf[:], ident[:, :])
            dgo(e, "ldc", cs_perm[:], permAB[:, :])
            dgo(e, "ldc", fc[:], fconsts[:, :])
            mark("consts", "ldc")
        blk.sync(pro_sync)

        def pro_dv(e):
            e.memset(onesc_bf[:], 1.0)
            e.memset(onesr_bf[:], 1.0)
            e.memset(onesf[:], 1.0)
            e.memset(negs[:], -0.5)
            e.memset(dens[:], 1.0)
            e.memset(nrm[:], 0.0)
            inc(e.memset(parena[:, 5 * 512:6 * 512], 1.0), "dv")
            mark("minit", "dv")
        blk.vector(pro_dv)

        def pro_ac(e):
            wt(e, "consts")
        blk.scalar(pro_ac)

        def pro_dv2(e):
            wt(e, "consts")
        blk.vector(pro_dv2)

        def pro_pe(e):
            wt(e, "consts")
            wt(e, "minit")
            v = inc(e.matmul(parena[0:32, 0:32], cs_idf[:, 0:32], cs_idf[:, 0:32],
                             start=True, stop=True), "pe")
            wait(e, "pe", v)
        blk.tensor(pro_pe)

        def load_sync(e, par, fam):
            for tt in range(par, NTT, 2):
                sg = stg if tt % 2 == 0 else stg2
                if tt >= 2:
                    wt(e, ("ldT", tt - 2))
                dgo(e, fam, sg[:], x_in[tt * 128:(tt + 1) * 128, :])
                mark(("ldL", tt), fam)
        blk.sync(lambda e: load_sync(e, 0, "ldx"))
        blk.scalar(lambda e: load_sync(e, 1, "ldy"))

        def load_pe(e):
            for tt in range(NTT):
                sg = stg if tt % 2 == 0 else stg2
                wt(e, ("ldL", tt))
                m = None
                for c in range(DC):
                    if c == 0 and tt >= 1:
                        wt(e, ("ldE4", tt - 1))
                    if c == 4 and tt >= 1:
                        wt(e, ("ldE", tt - 1))
                    m = e.transpose(parena[:, c * 512:c * 512 + 128],
                                    sg[:, c * 128:(c + 1) * 128], cs_idf[:])
                    if c == 3:
                        inc(m, "pe")
                        mark(("ldT4", tt), "pe")
                inc(m, "pe")
                mark(("ldT", tt), "pe")
            mark("ldone_pe", "pe")
        blk.tensor(load_pe)

        def load_dv(e):
            for tt in range(NTT):
                wt(e, ("ldT4", tt))
                m = None
                for c in range(DC):
                    if c == 4:
                        wt(e, ("ldT", tt))
                    m = e.tensor_copy(xs_c(c, tt * 128, (tt + 1) * 128),
                                      parena[:, c * 512:c * 512 + 128])
                    if c == 3:
                        inc(m, "dv")
                        mark(("ldE4", tt), "dv")
                inc(m, "dv")
                mark(("ldE", tt), "dv")
            mark("ldone", "dv")
        blk.vector(load_dv)

        def emit_ln_stats(tag, ready_keys):
            def dv_b(e):
                for k in ready_keys:
                    wt(e, k)
                for c in range(DC):
                    h = c % 2
                    if c >= 2:
                        wt(e, (tag, "mm", c - 2))
                    m = e.tensor_copy(xbf[:, h * T:(h + 1) * T], xs_c(c, 0, T))
                    inc(m, "dv")
                    mark((tag, "cast", c), "dv")
            blk.vector(dv_b)

            def ac_b(e):
                for c in range(DC):
                    h = c % 2
                    wt(e, (tag, "cast", c))
                    if c >= 2:
                        wt(e, (tag, "mm", c - 2))
                    m = e.activation(out=sqbf[:, h * T:(h + 1) * T],
                                     in_=xbf[:, h * T:(h + 1) * T], func=AF.Square)
                    inc(m, "ac")
                    mark((tag, "sq", c), "ac")
            blk.scalar(ac_b)

            def pe_b(e):
                for c in range(DC):
                    h = c % 2
                    wt(e, (tag, "sq", c))
                    m = None
                    for ts in range(4):
                        e.matmul(parena[0:1, ts * 512:(ts + 1) * 512], onesc_bf[:],
                                 xbf[:, h * T + ts * 512:h * T + (ts + 1) * 512],
                                 start=(c == 0), stop=(c == DC - 1), tile_position=(0, 0))
                        m = e.matmul(parena[0:1, (4 + ts) * 512:(5 + ts) * 512], onesc_bf[:],
                                     sqbf[:, h * T + ts * 512:h * T + (ts + 1) * 512],
                                     start=(c == 0), stop=(c == DC - 1), tile_position=(0, 0))
                    inc(m, "pe")
                    mark((tag, "mm", c), "pe")
            blk.tensor(pe_b)

            def ac2_b(e):
                wt(e, (tag, "mm", DC - 1))
                m = e.copy(out=rows[0:1, :], in_=parena[0:1, 0:4096])
                inc(m, "ac")
                mark((tag, "statc"), "ac")
            blk.scalar(ac2_b)

            def dv2_b(e):
                wt(e, (tag, "statc"))
                chain(e, e.tensor_scalar(out=rows[0:1, :], in0=rows[0:1, :],
                                         scalar1=1.0 / D, scalar2=None, op0=ALU.mult))
                chain(e, e.tensor_tensor(out=m2row[0:1, :], in0=rows[0:1, 0:T],
                                         in1=rows[0:1, 0:T], op=ALU.mult))
                m = e.tensor_tensor(out=rows[0:1, T:2 * T], in0=rows[0:1, T:2 * T],
                                    in1=m2row[0:1, :], op=ALU.subtract)
                inc(m, "dv")
                mark((tag, "var"), "dv")
            blk.vector(dv2_b)

            def ac3_b(e):
                wt(e, (tag, "var"))
                m = e.activation(out=rows[0:1, T:2 * T], in_=rows[0:1, T:2 * T],
                                 func=AF.Sqrt, bias=EPS)
                inc(m, "ac")
                mark((tag, "sdev"), "ac")
            blk.scalar(ac3_b)

            def dv3_b(e):
                wt(e, (tag, "sdev"))
                m = e.reciprocal(rows[0:1, T:2 * T], rows[0:1, T:2 * T])
                inc(m, "dv")
                mark((tag, "rstd"), "dv")
            blk.vector(dv3_b)

        def emit_ln_mat(tag, ln, dst, dstw, t0, t1, pre_keys=()):
            nts = (t1 - t0) // 512

            def pe_b(e):
                wt(e, (ln, "rstd"))
                for k in pre_keys:
                    wt(e, k)
                m = None
                for ts in range(nts):
                    e.matmul(bank(ts), onesf[0:1, :],
                             rows[0:1, t0 + ts * 512:t0 + (ts + 1) * 512],
                             start=True, stop=True, tile_position=(0, 0))
                    m = e.matmul(bank(nts + ts), onesf[0:1, :],
                                 rows[0:1, T + t0 + ts * 512:T + t0 + (ts + 1) * 512],
                                 start=True, stop=True, tile_position=(0, 0))
                inc(m, "pe")
                mark((tag, "bc"), "pe")
            blk.tensor(pe_b)

            def dv_b(e):
                wt(e, (tag, "bc"))
                for c in range(DC):
                    m = None
                    for ts in range(nts):
                        chain(e, e.tensor_tensor(
                            out=lntmp[:, :],
                            in0=xs_c(c, t0 + ts * 512, t0 + (ts + 1) * 512),
                            in1=bank(ts), op=ALU.subtract))
                        m = e.tensor_tensor(
                            out=dst[:, c * dstw + ts * 512:c * dstw + (ts + 1) * 512],
                            in0=lntmp[:, :], in1=bank(nts + ts), op=ALU.mult)
                        chain(e, m)
                    mark((tag, "mat", c), "dv")
                mark((tag, "matdone"), "dv")
            blk.vector(dv_b)

        def emit_ffn(ph, gate_keys=()):
            ln = "ln%d" % ph
            for g in range(NTG):
                t0 = g * TG
                ftag = "f%d%d" % (ph, g)
                mtag = ftag + "m"
                prevf = "f%d%d" % (ph, g - 1) if g > 0 else \
                        ("f1%d" % (NTG - 1) if ph == 3 else None)
                pre = []
                if g > 0:
                    pre = [("f%d%d" % (ph, g - 1), "res", 1, DC - 1)]
                emit_ln_mat(mtag, ln, xlnf, TG, t0, t0 + TG, pre_keys=pre)

                def gp_b(e, ftag=ftag, prevf=prevf, g=g, ph=ph):
                    for k_ in gate_keys if g == 0 else ():
                        wt(e, k_)
                    if g == 0:
                        wt(e, (ln, "statc"))
                    for p in range(NH // 2):
                        if p >= 2:
                            wt(e, (ftag, "gu", 2 * p - 3))
                        elif prevf is not None:
                            wt(e, (prevf, "gu", NH - 4 + 2 * p + 1))
                        dgo(e, "lwg", wgu_s[:, (p % 2) * 4 * D:(p % 2 + 1) * 4 * D],
                            wguL[2 * p:2 * p + 2, :, :].rearrange("a p e -> p a e"))
                        mark((ftag, "wgu", p), "lwg")
                    for ts in range(2):
                        for p in range(NH // 4):
                            if p >= 2:
                                wt(e, (ftag, "dn", ts, 4 * p - 5))
                            elif ts == 1:
                                wt(e, (ftag, "dn", 0, NH - 8 + 4 * p + 3))
                            elif prevf is not None:
                                wt(e, (prevf, "dn", 1, NH - 8 + 4 * p + 3))
                            dgo(e, "lwd", wd_s[:, (p % 2) * 4 * D:(p % 2 + 1) * 4 * D],
                                wdL[4 * p:4 * p + 4, :, :].rearrange("a p e -> p a e"))
                            mark((ftag, "wd", ts, p), "lwd")
                blk.gpsimd(gp_b)

                def pe_b(e, ftag=ftag, mtag=mtag):
                    for k in range(NH):
                        wt(e, (ftag, "wgu", k // 2))
                        if k == 0:
                            wt(e, (mtag, "matdone"))
                        if k >= 2:
                            wt(e, (ftag, "silu", k - 2))
                            wt(e, (ftag, "hmul", k - 2))
                        for c in range(DC):
                            for ts in range(2):
                                m = e.matmul(
                                    bank(4 * (k % 2) + ts),
                                    wgu_s[:, ((k // 2) % 2) * 4 * D + (k % 2) * 2 * D + c * 128:
                                          ((k // 2) % 2) * 4 * D + (k % 2) * 2 * D + (c + 1) * 128],
                                    xlnf[:, c * TG + ts * 512:c * TG + (ts + 1) * 512],
                                    start=(c == 0), stop=(c == DC - 1))
                        inc(m, "pe")
                        mark((ftag, "gmm", k), "pe")
                        for c in range(DC):
                            for ts in range(2):
                                m = e.matmul(
                                    bank(4 * (k % 2) + 2 + ts),
                                    wgu_s[:, ((k // 2) % 2) * 4 * D + (k % 2) * 2 * D + D + c * 128:
                                          ((k // 2) % 2) * 4 * D + (k % 2) * 2 * D + D + (c + 1) * 128],
                                    xlnf[:, c * TG + ts * 512:c * TG + (ts + 1) * 512],
                                    start=(c == 0), stop=(c == DC - 1))
                        inc(m, "pe")
                        mark((ftag, "gu", k), "pe")
                    for ts in range(2):
                        for k in range(NH):
                            wt(e, (ftag, "wd", ts, k // 4))
                            if ts == 0 and k == 0:
                                wt(e, (ftag, "hmul", NH - 1))
                                wt(e, (ftag, "hmul", NH - 2))
                            if ts == 1 and k == 0:
                                for dd in range(DC):
                                    wt(e, (ftag, "res", 0, dd))
                            for dout in range(DC):
                                m = e.matmul(
                                    bank(dout),
                                    wd_s[:, ((k // 4) % 2) * 4 * D + (k % 4) * D + dout * 128:
                                          ((k // 4) % 2) * 4 * D + (k % 4) * D + (dout + 1) * 128],
                                    hbuf[:, k * TG + ts * 512:k * TG + (ts + 1) * 512],
                                    start=(k == 0), stop=(k == NH - 1))
                            inc(m, "pe")
                            mark((ftag, "dn", ts, k), "pe")
                blk.tensor(pe_b)

                def ac_b(e, ftag=ftag):
                    for k in range(NH):
                        wt(e, (ftag, "gmm", k))
                        if k >= 2:
                            wt(e, (ftag, "hmul", k - 2))
                        m = e.activation(out=silu_g[:, (k % 2) * TG:(k % 2 + 1) * TG],
                                         in_=bank(4 * (k % 2), n=1024), func=AF.Silu,
                                         scale=WS)
                        inc(m, "ac")
                        mark((ftag, "silu", k), "ac")
                blk.scalar(ac_b)

                def dv_b(e, ftag=ftag, t0=t0):
                    for k in range(NH):
                        wt(e, (ftag, "silu", k))
                        wt(e, (ftag, "gu", k))
                        m = e.tensor_tensor(out=hbuf[:, k * TG:(k + 1) * TG],
                                            in0=silu_g[:, (k % 2) * TG:(k % 2 + 1) * TG],
                                            in1=bank(4 * (k % 2) + 2, n=1024), op=ALU.mult)
                        inc(m, "dv")
                        mark((ftag, "hmul", k), "dv")
                    for ts in range(2):
                        wt(e, (ftag, "dn", ts, NH - 1))
                        for dout in range(DC):
                            m = e.scalar_tensor_tensor(
                                out=xs_c(dout, t0 + ts * 512, t0 + (ts + 1) * 512),
                                in0=bank(dout), scalar=0.5 * WS * WS,
                                in1=xs_c(dout, t0 + ts * 512, t0 + (ts + 1) * 512),
                                op0=ALU.mult, op1=ALU.add)
                            inc(m, "dv")
                            mark((ftag, "res", ts, dout), "dv")
                blk.vector(dv_b)
            mark(("ffn%d" % ph, "dv"), "dv")
            mark(("ffn%d" % ph, "pe"), "pe")

        emit_ln_stats("ln1", ["ldone"])
        emit_ffn(1)

        if dbg == "ffn1":
            def dump1(e):
                wt(e, ("ffn1", "dv"))
                dgo(e, "dbg", dbg_out[:, 0:DC * T], xs[:])
                dgo(e, "dbg", dbg_out2[0:1, 0:2 * T], rows[:, :])
                dgo(e, "dbg", dbg_out2[0:1, 2 * T:3 * T], m2row[:, :])
                wait(e, "dbg", val("dbg"))
            blk.sync(dump1)
            return nc

        emit_ln_stats("ln2", [("ffn1", "dv")])

        def spill_x1(e):
            wt(e, ("ln2", "cast", DC - 1))
            wt(e, ("ffn1", "dv"))
            dgo(e, "sp1", x1sp[:, :], xs[:])
            mark("x1spilled", "sp1")
        blk.sync(spill_x1)

        def spgate(e):
            wt(e, "x1spilled")
            wait(e, "dv", val("dv"))
            inc(e.nop(), "dv")
            mark("spgate", "dv")
        blk.vector(spgate)

        emit_ln_mat("ln2m", "ln2", xln2, T, 0, T, pre_keys=["spgate"])

        if dbg == "xln2":
            def dumpx(e):
                wt(e, ("ln2m", "matdone"))
                dgo(e, "dbg", dbg_out[:, 0:DC * T], xln2[:])
                e.wait_ge(sems["dbg"], cnt["dbg"])
            blk.gpsimd(dumpx)
            return

        def pw_gp(e, rng):
            if 0 in rng:
                wt(e, ("ffn1", "pe"))
            for i in rng:
                wl = (wkL, wvL, wqL)[i]
                if i > 0:
                    wt(e, ("pj", "wfree", i - 1))
                dgo(e, "lww", wfull[:, :], wl[:, :, :].rearrange("c p e -> p c e"))
                mark(("pj", "w", i), "lww")
                if i > 0:
                    wt(e, ("pj", "bfree", i - 1))
                dgo(e, "lwb", biasb[:, :], biases[i:i + 1, :])
                mark(("pj", "b", i), "lwb")
        blk.gpsimd(lambda e: pw_gp(e, (0, 1)))

        def pwo_gp(e):
            wt(e, ("pj", "bfree", 2))
            dgo(e, "lwb", biasb[:, :], biases[3:4, :])
            mark(("pj", "b", 3), "lwb")

        def wo_gp(e):
            wt(e, ("qp", "ropedone"))

            def wo_load(qt, dout):
                idx = qt * DC + dout
                if idx >= 2:
                    wt(e, ("opmm", (idx - 2) // DC, (idx - 2) % DC))
                dgo(e, "lwo", woS[:, (idx % 2) * D:(idx % 2 + 1) * D],
                    woL[:, :, dout * 128:(dout + 1) * 128].rearrange("c p j -> p c j"))
                mark(("wo", qt, dout), "lwo")

            if not SKIP_OP:
                wo_load(0, 0)
                wo_load(0, 1)
            for qt in range(NQC_E):
                if not SKIP_OP:
                    for dout in range(2, DC):
                        wo_load(qt, dout)
                    if qt + 1 < NQC_E:
                        wo_load(qt + 1, 0)
                        wo_load(qt + 1, 1)

        def pw_sync(e):
            wt(e, ("ln2", "statc"))
            dgo(e, "cabq", cab[:, :], cak[:, :])
            mark("cab_k", "cabq")
        blk.sync(pw_sync)

        def emit_proj_fm(idx, tagp, dst, prev_done_key, cab_key):
            def pe_b(e):
                wt(e, ("pj", "w", idx))
                wt(e, ("pj", "b", idx))
                wt(e, prev_done_key)
                for pc in range(DC):
                    if pc >= 1:
                        wt(e, (tagp, "qev", pc - 1))
                    for c in range(DC):
                        for ts in range(4):
                            e.matmul(bank(ts),
                                     wfull[:, c * D + pc * 128:c * D + (pc + 1) * 128],
                                     xln2[:, c * T + ts * 512:c * T + (ts + 1) * 512],
                                     start=(c == 0), stop=False)
                    m = None
                    for ts in range(4):
                        m = e.matmul(bank(ts), biasb[:, pc * 128:(pc + 1) * 128],
                                     onesr_bf[0:1, ts * 512:(ts + 1) * 512],
                                     start=False, stop=True)
                    inc(m, "pe")
                    mark((tagp, "qmm", pc), "pe")
                    wt(e, (tagp, "qev", pc))
                    for th in range(2):
                        if pc >= 1 or th >= 1:
                            prev = (tagp, "rope", pc, th - 1) if th >= 1 else (tagp, "rope", pc - 1, 1)
                            wt(e, prev)
                        for hh in range(2):
                            e.matmul(bank(4 + hh), cs_perm[:, 0:128],
                                     qraw[:, th * KB + hh * 512:th * KB + (hh + 1) * 512],
                                     start=True, stop=True)
                            m = e.matmul(bank(6 + hh), cs_perm[:, 128:256],
                                         qraw[:, th * KB + hh * 512:th * KB + (hh + 1) * 512],
                                         start=True, stop=True)
                        inc(m, "pe")
                        mark((tagp, "perm", pc, th), "pe")
                mark(("pj", "wfree", idx), "pe")
                mark(("pj", "bfree", idx), "pe")
            blk.tensor(pe_b)

            def ac_b(e):
                for pc in range(DC):
                    wt(e, (tagp, "qmm", pc))
                    if pc >= 1:
                        wt(e, (tagp, "perm", pc - 1, 1))
                    m = e.activation(out=qraw[:, :], in_=parena[:, 0:2048], func=AF.Copy)
                    inc(m, "ac")
                    mark((tagp, "qev", pc), "ac")
            blk.scalar(ac_b)

            def dv_b(e):
                wt(e, cab_key)
                for pc in range(DC):
                    for th in range(2):
                        wt(e, (tagp, "perm", pc, th))
                        m = None
                        for hh in range(2):
                            sl = th * KB + hh * 512
                            e.tensor_tensor(out=rtmp[:, 0:512], in0=bank(4 + hh),
                                            in1=cab[:, sl:sl + 512], op=ALU.mult)
                            chain(e, e.tensor_tensor(out=rtmp[:, 512:1024], in0=bank(6 + hh),
                                                     in1=cab[:, T + sl:T + sl + 512],
                                                     op=ALU.mult))
                            m = e.tensor_tensor(out=dst[:, pc * T + sl:pc * T + sl + 512],
                                                in0=rtmp[:, 0:512], in1=rtmp[:, 512:1024],
                                                op=ALU.add)
                            chain(e, m)
                        mark((tagp, "rope", pc, th), "dv")
                mark((tagp, "ropedone"), "dv")
            blk.vector(dv_b)

        emit_proj_fm(0, "kp", kstg, ("ln2m", "matdone"), "cab_k")

        def cabq_sync(e):
            wt(e, ("kp", "ropedone"))
            dgo(e, "cabq", cab[:, :], caq[:, :])
            mark("cab_q", "cabq")
        blk.sync(cabq_sync)

        def vproj_pe(e):
            wt(e, ("pj", "w", 1))
            wt(e, ("pj", "b", 1))
            wt(e, ("kp", "ropedone"))
            for tt in range(NTT):
                bs = 4 + 2 * (tt % 2)
                if tt >= 2:
                    wt(e, ("pj", "vev", tt - 2))
                for c in range(DC):
                    for eh in range(2):
                        e.matmul(bank(bs + eh),
                                 xln2[:, c * T + tt * 128:c * T + (tt + 1) * 128],
                                 wfull[:, c * D + eh * 512:c * D + (eh + 1) * 512],
                                 start=(c == 0), stop=False)
                for eh in range(2):
                    m = e.matmul(bank(bs + eh), onesr_bf[0:1, 0:128],
                                 biasb[:, eh * 512:(eh + 1) * 512], start=False, stop=True)
                inc(m, "pe")
                mark(("pj", "vmm", tt), "pe")
            mark(("pj", "wfree", 1), "pe")
            mark(("pj", "bfree", 1), "pe")
        blk.tensor(vproj_pe)

        def vproj_ac(e):
            for tt in range(NTT):
                bs = 4 + 2 * (tt % 2)
                wt(e, ("pj", "vmm", tt))
                m = e.activation(out=vstg[:, tt * KB:(tt + 1) * KB],
                                 in_=parena[:, bs * 512:(bs + 2) * 512], func=AF.Copy)
                inc(m, "ac")
                mark(("pj", "vev", tt), "ac")
            mark("vdone", "ac")
        blk.scalar(vproj_ac)

        emit_proj_fm(2, "qp", qrope, "vdone", "cab_q")

        if dbg == "qkv":
            def dumpq(e):
                wt(e, ("qp", "ropedone"))
                wt(e, "vdone")
                dgo(e, "dbg", dbg_out[:, 0:DC * T], qrope[:])
                dgo(e, "dbg", dbg_out[:, DC * T:2 * DC * T], kstg[:])
                dgo(e, "dbg", dbg_out2[:, 0:16 * KB], vstg[:])
                e.wait_ge(sems["dbg"], cnt["dbg"])
            blk.gpsimd(dumpq)
            return

        def ag_sync(e):
            wt(e, ("kp", "ropedone"))
            dgo(e, "snd", k_snd[:, :], kstg[:, :])
            mark("kout", "snd")
            wt(e, "vdone")
            dgo(e, "snd", v_snd[:, :], vstg[:, :])
            mark("vout", "snd")
        blk.sync(ag_sync)

        def agK_gp(e):
            wt(e, "kout")
            m = e.collective_compute(
                "AllGather", ALU.bypass,
                replica_groups=[[0, 1], [2, 3], [4, 5], [6, 7]],
                ins=[k_snd.ap().opt()], outs=[k_rcv.ap().opt()])
            inc(m, "cc")
            mark("agK", "cc")
        blk.gpsimd(agK_gp)

        blk.gpsimd(lambda e: pw_gp(e, (2,)))

        def agV_gp(e):
            wt(e, "vout")
            m = e.collective_compute(
                "AllGather", ALU.bypass,
                replica_groups=[[0, 1], [2, 3], [4, 5], [6, 7]],
                ins=[v_snd.ap().opt()], outs=[v_rcv.ap().opt()])
            inc(m, "cc")
            mark("agV", "cc")
        blk.gpsimd(agV_gp)

        blk.gpsimd(pwo_gp)

        def ag_in_sync(e):
            wt(e, "agK")
            for c in range(4):
                for r in range(2):
                    dgo(e, "kin", kfull[:, c * S + r * T:c * S + r * T + T],
                        k_rcv[r * 128:(r + 1) * 128, c * T:(c + 1) * T])
                mark(("kin", c), "kin")
            def vin_chunk(tc):
                r0 = 128 * (tc // 16)
                sc = (tc % 16) * KB
                srcv = v_rcv[r0:r0 + 128, sc:sc + KB]
                for h, fam in ((0, "vin"), (1, "vin2")):
                    if tc < 31:
                        dst = vfA[:, tc * 1032:(tc + 1) * 1032].rearrange(
                            "k (p c) -> k p c", p=8)[:, :, h * 65:h * 65 + 64]
                        dgo(e, fam, dst, srcv.rearrange(
                            "k (p c) -> k p c", p=8)[:, :, h * 64:(h + 1) * 64])
                    else:
                        dgo(e, fam, vfA[:, 31 * 1032:31 * 1032 + 6 * 129].rearrange(
                            "k (p c) -> k p c", p=6)[:, :, h * 65:h * 65 + 64],
                            srcv[:, 0:768].rearrange(
                                "k (p c) -> k p c", p=6)[:, :, h * 64:(h + 1) * 64])
                        dgo(e, fam, vfB[:, :].rearrange(
                            "k (p c) -> k p c", p=2)[:, :, h * 65:h * 65 + 64],
                            srcv[:, 768:1024].rearrange(
                                "k (p c) -> k p c", p=2)[:, :, h * 64:(h + 1) * 64])

            wt(e, "agV")
            for tc in range(16):
                vin_chunk(tc)
            wt(e, ("qp", "qmm", DC - 1))
            for c in range(4, 6):
                for r in range(2):
                    dgo(e, "kin", kfull[:, c * S + r * T:c * S + r * T + T],
                        k_rcv[r * 128:(r + 1) * 128, c * T:(c + 1) * T])
                mark(("kin", c), "kin")
            wt(e, ("qp", "ropedone"))
            for c in range(6, 8):
                for r in range(2):
                    dgo(e, "kin", kfull[:, c * S + r * T:c * S + r * T + T],
                        k_rcv[r * 128:(r + 1) * 128, c * T:(c + 1) * T])
                mark(("kin", c), "kin")
            mark("kvin", "kin")
            for tc in range(16, 32):
                vin_chunk(tc)
            mark("vin", "vin")
            mark("vin2", "vin2")
        blk.sync(ag_in_sync)

        def von_gp(e):
            vA3 = vfA[:, 0:254 * 129].rearrange("p (j c) -> p j c", c=129)
            wt(e, "vout")
            m = e.memset(vA3[:, 0:127, 64:65], 1.0)
            inc(m, "gp")
            wt(e, ("qp", "ropedone"))
            e.memset(vA3[:, 127:254, 64:65], 1.0)
            m = e.memset(vfB[:, :].rearrange("p (j c) -> p j c", c=129)[:, :, 64:65], 1.0)
            inc(m, "gp")
            mark("vones", "gp")
        blk.gpsimd(von_gp)

        def nrm_ac(e):
            wt(e, ("qp", "ropedone"))
            for pc in range(DC):
                h = pc % 2
                if pc >= 2:
                    wt(e, ("nqm", pc - 2))
                m = e.activation(out=nsq[:, h * T:(h + 1) * T],
                                 in_=qrope[:, pc * T:(pc + 1) * T], func=AF.Square)
                inc(m, "ac")
                mark(("nq", pc), "ac")
            for pc in range(DC):
                wt(e, ("kin", pc))
                if pc == 0:
                    wt(e, ("nqm", DC - 1))
                else:
                    wt(e, ("nkm", pc - 1))
                m = e.activation(out=nsq[:, 0:S], in_=kfull[:, pc * S:(pc + 1) * S],
                                 func=AF.Square)
                inc(m, "ac")
                mark(("nk", pc), "ac")
        blk.scalar(nrm_ac)

        def nrm_pe(e):
            for pc in range(DC):
                h = pc % 2
                R = 32 * (pc // 2)
                wt(e, ("nq", pc))
                m = None
                for ts in range(4):
                    m = e.matmul(parena[R:R + 1, (h * 4 + ts) * 512:(h * 4 + ts + 1) * 512],
                                 onesc_bf[:], nsq[:, (h * 4 + ts) * 512:(h * 4 + ts + 1) * 512],
                                 start=True, stop=True, tile_position=(0, R))
                inc(m, "pe")
                mark(("nqm", pc), "pe")
            for pc in range(DC):
                R = 32 * (pc % 4)
                wt(e, ("nk", pc))
                if pc < 4:
                    wt(e, ("nqr", 2 * pc))
                    wt(e, ("nqr", 2 * pc + 1))
                else:
                    wt(e, ("nkr", pc - 4))
                m = None
                for ts in range(8):
                    m = e.matmul(parena[R:R + 1, ts * 512:(ts + 1) * 512],
                                 onesc_bf[:], nsq[:, ts * 512:(ts + 1) * 512],
                                 start=True, stop=True, tile_position=(0, R))
                inc(m, "pe")
                mark(("nkm", pc), "pe")
            wt(e, ("nsum",))
            m = e.matmul(parena[:, 0:8], negs[:, :], nrm[0:1, 32:40],
                         start=True, stop=True)
            inc(m, "pe")
            mark("negCm", "pe")
        blk.tensor(nrm_pe)

        def nrm_red(e, par, sem):
            ax = mybir.AxisListType.X
            for pc in range(par, DC, 1 if par == 0 else 9):
                h = pc % 2
                R = 32 * (pc // 2)
                wt(e, ("nqm", pc))
                m = e.tensor_reduce(out=nrm[0:1, pc:pc + 1],
                                    in_=parena[R:R + 1, h * 2048:(h + 1) * 2048],
                                    axis=ax, op=ALU.max)
                wait(e, sem, inc(m, sem))
                mark(("nqr", pc), sem)
            for pc in range(par, DC, 1 if par == 0 else 9):
                R = 32 * (pc % 4)
                wt(e, ("nkm", pc))
                m = e.tensor_reduce(out=nrm[0:1, 16 + pc:17 + pc],
                                    in_=parena[R:R + 1, 0:4096],
                                    axis=ax, op=ALU.max)
                wait(e, sem, inc(m, sem))
                mark(("nkr", pc), sem)
        blk.vector(lambda e: nrm_red(e, 0, "dv"))

        def nrm_gp(e):
            wt(e, ("nkr", DC - 1))
            m = e.tensor_tensor(out=nrm[0:1, 32:48], in0=nrm[0:1, 0:16],
                                in1=nrm[0:1, 16:32], op=ALU.add)
            inc(m, "gp")
            mark(("nsum",), "gp")
        blk.gpsimd(nrm_gp)

        def nrm_ac2(e):
            wt(e, "negCm")
            m = e.copy(out=negC[:, :], in_=parena[:, 0:8])
            inc(m, "ac")
            mark("negCr", "ac")
        blk.scalar(nrm_ac2)

        blk.gpsimd(wo_gp)

        if dbg == "kv":
            def dumpkv(e):
                wt(e, "kvin")
                wt(e, "vin")
                wt(e, "negCr")
                dgo(e, "dbg", dbg_out[:, :], kfull[:])
                dgo(e, "dbg", dbg_out2[:, 0:31 * 1032 + 6 * 129], vfA[:])
                dgo(e, "dbg", dbg_out2[:, KC * KB - NPAIR:KC * KB], negC[:])
                dgo(e, "dbg", dbg_out2[0:1, KC * KB - 72:KC * KB - 8], nrm[:])
                e.wait_ge(sems["dbg"], cnt["dbg"])
            blk.gpsimd(dumpkv)
            return

        OPB = 7

        def attn_pe(e):
            wt(e, "kvin")
            wt(e, "vin")
            wt(e, "vin2")
            wt(e, "vones")
            wt(e, "negCr")

            def emit_outproj(e, qt, dout):
                wt(e, ("pj", "b", 3))
                wt(e, ("epiTT", qt * NPAIR + NPAIR - 1))
                idx = qt * DC + dout
                wt(e, ("wo", qt, dout))
                if dout >= 1:
                    wt(e, ("opev", qt, dout - 1))
                for c in range(DC):
                    e.matmul(bank(OPB),
                             woS[:, (idx % 2) * D + c * 128:(idx % 2) * D + (c + 1) * 128],
                             attn_fm[:, c * QT:(c + 1) * QT],
                             start=(c == 0), stop=False)
                m = e.matmul(bank(OPB), biasb[:, dout * 128:(dout + 1) * 128],
                             onesr_bf[0:1, 0:QT], start=False, stop=True)
                inc(m, "pe")
                mark(("opmm", qt, dout), "pe")

            for qt in range(NQC_E):
                for pr in range(NPAIR_E):
                    it0 = (qt * NPAIR + pr)
                    pvb = 4 + 2 * (pr % 2)
                    dnb = pvb + 1
                    for kc in range(KC + 1):
                        if kc < KC:
                            sb = 2 * (kc % 2)
                            if kc >= 2:
                                wt(e, ("exp", it0, kc - 2))
                            elif it0 > 0 and kc == 0:
                                wt(e, ("exp", it0 - 1, KC - 2))
                            elif it0 > 0:
                                wt(e, ("exp", it0 - 1, KC - 1))
                            if kc == 0:
                                if it0 >= 2:
                                    wt(e, ("epiTT", it0 - 2))
                                if pr == 1 and qt > 0 and not SKIP_OP:
                                    wt(e, ("opev", qt - 1, DC - 1))
                            e.matmul(parena[:, sb * 512:sb * 512 + 512],
                                     kfull[0:64, pr * S + kc * 128:pr * S + (kc + 1) * 128],
                                     qrope[0:64, pr * T + qt * QT:pr * T + (qt + 1) * QT],
                                     start=True, stop=True, tile_position=(0, 0))
                            m = e.matmul(parena[:, sb * 512 + 512:sb * 512 + 1024],
                                         kfull[64:128, pr * S + kc * 128:pr * S + (kc + 1) * 128],
                                         qrope[64:128, pr * T + qt * QT:pr * T + (qt + 1) * QT],
                                         start=True, stop=True, tile_position=(64, 0))
                            inc(m, "pe")
                            mark(("qk", it0, kc), "pe")
                        if kc >= 1:
                            kp = kc - 1
                            pb = (kp % 2) * KB
                            wt(e, ("exp", it0, kp))
                            e.matmul(parena[0:65, pvb * 512:pvb * 512 + 512],
                                     vf(kp, pr)[:, 0:65], pT[:, pb:pb + 512],
                                     start=(kp == 0), stop=(kp == KC - 1),
                                     tile_position=(0, 0))
                            e.matmul(parena[64:128, dnb * 512:dnb * 512 + 512],
                                     vf(kp, pr)[:, 65:129], pT[:, pb + 512:pb + 1024],
                                     start=(kp == 0), stop=(kp == KC - 1),
                                     tile_position=(0, 64))
                            m = e.matmul(parena[0:1, dnb * 512:dnb * 512 + 512], onesc_bf[:],
                                         pT[:, pb + 512:pb + 1024],
                                         start=(kp == 0), stop=(kp == KC - 1),
                                         tile_position=(0, 0))
                            inc(m, "pe")
                            mark(("pv", it0, kp), "pe")
                        if (not SKIP_OP and pr == 0 and qt > 0 and kc >= 4
                                and kc % 2 == 0 and (kc - 4) // 2 < DC):
                            emit_outproj(e, qt - 1, (kc - 4) // 2)
                if SKIP_OP or qt != NQC_E - 1:
                    continue
                for dout in range(DC):
                    emit_outproj(e, qt, dout)
            mark("attn_pe", "pe")
        blk.tensor(attn_pe)

        def attn_ac(e):
            for qt in range(NQC_E):
                for pr in range(NPAIR_E):
                    it0 = (qt * NPAIR + pr)
                    for kc in range(KC):
                        sb = 2 * (kc % 2)
                        pb = (kc % 2) * KB
                        wt(e, ("qk", it0, kc))
                        if kc >= 2:
                            wt(e, ("pv", it0, kc - 2))
                        m = e.activation(out=pT[:, pb:pb + KB],
                                         in_=parena[:, sb * 512:sb * 512 + KB],
                                         func=AF.Exp, bias=negC[:, pr:pr + 1], scale=1.0)
                        inc(m, "ac")
                        mark(("exp", it0, kc), "ac")

        blk.scalar(attn_ac)

        def attn_dv(e):
            for qt in range(NQC_E):
                for pr in range(NPAIR_E):
                    it0 = (qt * NPAIR + pr)
                    wt(e, ("pv", it0, KC - 1))
                    if it0 > 0:
                        wt(e, ("bc", it0 - 1))
                    dnb = 5 + 2 * (pr % 2)
                    pvb = dnb - 1
                    chain(e, e.tensor_copy(dens[64:65, :],
                                           parena[64:65, pvb * 512:(pvb + 1) * 512]))
                    m = e.tensor_copy(dens[0:1, :], parena[0:1, dnb * 512:(dnb + 1) * 512])
                    inc(m, "dv")
                    mark(("denc", it0), "dv")
                    wait(e, "dv", val("dv"))
                    m = e.reciprocal(recips[:, :], dens[:, :])
                    inc(m, "dv")
                    mark(("recip", it0), "dv")
                    wt(e, ("bc", it0))
                    if pr == 0 and qt > 0:
                        wt(e, ("opmm", qt - 1, DC - 1))
                    chain(e, e.tensor_tensor(out=attn_fm[0:64, pr * QT:(pr + 1) * QT],
                                             in0=parena[0:64, pvb * 512:(pvb + 1) * 512],
                                             in1=rbs[0:64, :], op=ALU.mult))
                    m = e.tensor_tensor(out=attn_fm[64:128, pr * QT:(pr + 1) * QT],
                                        in0=parena[64:128, dnb * 512:(dnb + 1) * 512],
                                        in1=rbs[64:128, :], op=ALU.mult)
                    inc(m, "dv")
                    mark(("epiTT", it0), "dv")
                if SKIP_OP:
                    continue
                wt(e, ("x2ld", qt))
                for dout in range(DC):
                    wt(e, ("opmm", qt, dout))
                    m = e.tensor_tensor(out=x2stg[:, dout * QT:(dout + 1) * QT],
                                        in0=bank(OPB),
                                        in1=x2stg[:, dout * QT:(dout + 1) * QT],
                                        op=ALU.add)
                    inc(m, "dv")
                    mark(("opev", qt, dout), "dv")
                mark(("x2rdy", qt), "dv")
            mark("attn_dv", "dv")
        blk.vector(attn_dv)

        def attn_sync(e):
            wt(e, "x1spilled")
            wt(e, ("qp", "ropedone"))
            for qt in range(NQC_E):
                if not SKIP_OP:
                    if qt >= 1:
                        wt(e, ("x2st", qt - 1))
                    dgo(e, "x2l", x2stg[:, :],
                        x1sp[:, :].rearrange("p (c t) -> p c t", c=DC)[:, :, qt * QT:(qt + 1) * QT])
                    mark(("x2ld", qt), "x2l")
                for pr in range(NPAIR_E):
                    it0 = qt * NPAIR + pr
                    wt(e, ("recip", it0))
                    if it0 > 0:
                        wt(e, ("epiTT", it0 - 1))
                    dgo(e, "bcd", rbd[0:1, :], recips[64:65, :])
                    dgo(e, "bcd", rbd[1:2, :], recips[0:1, :])
                    dgo(e, "bcd", rbs[0:64, :], rbd[0:1, :].broadcast_to((64, QT)))
                    dgo(e, "bcd", rbs[64:128, :], rbd[1:2, :].broadcast_to((64, QT)))
                    mark(("bc", it0), "bcd")
                if not SKIP_OP:
                    wt(e, ("x2rdy", qt))
                    dgo(e, "x2s",
                        x2sp[:, :].rearrange("p (c t) -> p c t", c=DC)[:, :, qt * QT:(qt + 1) * QT],
                        x2stg[:, :])
                    mark(("x2st", qt), "x2s")
            mark("x2done", "x2s")
        blk.sync(attn_sync)

        if dbg in ("attn1", "attn1b"):
            def dumpa1(e):
                wt(e, ("epiTT", NPAIR_E - 1))
                dgo(e, "dbg", dbg_out[:, 0:DC * QT], attn_fm[:])
                dgo(e, "dbg", dbg_out2[0:65, 0:QT], dens[:])
                e.wait_ge(sems["dbg"], cnt["dbg"])
            blk.gpsimd(dumpa1)
            return

        if dbg == "attn2":
            def dumpa2(e):
                wt(e, ("x2st", 0))
                dgo(e, "dbg", dbg_out[:, 0:DC * QT], x2stg[:])
                e.wait_ge(sems["dbg"], cnt["dbg"])
            blk.gpsimd(dumpa2)
            return

        if dbg == "attn":
            def dumpa(e):
                wt(e, "x2done")
                dgo(e, "dbg", xs[:, :], x2sp[:, :])
                wait(e, "dbg", val("dbg"))
                dgo(e, "dbg", dbg_out[:, 0:DC * T], xs[:])
                wait(e, "dbg", val("dbg"))
            blk.sync(dumpa)
            return nc

        def x2_reload(e):
            wt(e, "x2done")
            dgo(e, "x2l", xs[:, :], x2sp[:, :])
            mark("x2full", "x2l")
        blk.sync(x2_reload)

        def x2gate(e):
            wt(e, "x2full")
            wt(e, "attn_dv")
            inc(e.nop(), "dv")
            mark("x2ok", "dv")
        blk.vector(x2gate)

        emit_ln_stats("ln3", ["x2ok"])
        emit_ffn(3, gate_keys=["attn_pe", "attn_dv"])

        def out_pe(e):
            wt(e, ("ffn3", "dv"))
            for tt in range(NTT):
                m = None
                for c in range(DC):
                    if c == 0 and tt >= 1:
                        wt(e, ("oev4", tt - 1))
                    if c == 4 and tt >= 1:
                        wt(e, ("oev", tt - 1))
                    m = e.transpose(parena[:, c * 512:c * 512 + 128],
                                    xs_c(c, tt * 128, (tt + 1) * 128), cs_idf[:])
                    if c == 3:
                        inc(m, "pe")
                        mark(("oT4", tt), "pe")
                inc(m, "pe")
                mark(("oT", tt), "pe")
        blk.tensor(out_pe)

        def out_dv(e):
            for tt in range(NTT):
                sg = stg if tt % 2 == 0 else stg2
                if tt >= 2:
                    wt(e, ("ost", tt - 2))
                wt(e, ("oT4", tt))
                m = None
                for c in range(DC):
                    if c == 4:
                        wt(e, ("oT", tt))
                    m = e.tensor_copy(sg[:, c * 128:(c + 1) * 128],
                                      parena[:, c * 512:c * 512 + 128])
                    if c == 3:
                        inc(m, "dv")
                        mark(("oev4", tt), "dv")
                inc(m, "dv")
                mark(("oev", tt), "dv")
        blk.vector(out_dv)

        def out_sync(e, par, fam):
            for tt in range(par, NTT, 2):
                sg = stg if tt % 2 == 0 else stg2
                wt(e, ("oev", tt))
                dgo(e, fam, y_out[tt * 128:(tt + 1) * 128, :], sg[:])
                mark(("ost", tt), fam)
            wait(e, "sty", val("sty"))
            wait(e, "sty2", val("sty2"))
        blk.sync(lambda e: out_sync(e, 0, "sty"))
        blk.gpsimd(lambda e: out_sync(e, 1, "sty2"))

    return None


def _host_tables(core):
    inv = 10000.0 ** (-np.arange(32, dtype=np.float64) / 32.0)
    pos = (core % 2) * T + np.arange(T, dtype=np.float64)
    f = inv[:, None] * pos[None, :]
    cos, sin = np.cos(f), np.sin(f)
    ca64 = np.concatenate([cos, sin], axis=0)
    cb64 = np.concatenate([-sin, cos], axis=0)
    CA = np.concatenate([ca64, ca64], axis=0)
    CB = np.concatenate([cb64, cb64], axis=0)
    caq = np.concatenate([CA, CB], axis=1) * 0.125
    cak = np.concatenate([CA, CB], axis=1)
    return caq.astype(np.float32), cak.astype(np.float32)


def _perm_mats():
    pa = np.zeros((128, 128), dtype=np.float32)
    pb = np.zeros((128, 128), dtype=np.float32)
    for hp in range(2):
        for i in range(64):
            src = 64 * hp + 2 * (i % 32)
            pa[src, 64 * hp + i] = 1.0
            pb[src + 1, 64 * hp + i] = 1.0
    return np.concatenate([pa, pb], axis=1).astype(ml_dtypes.bfloat16)


def kernel(L0_attn_x, L0_attn_Wq, L0_attn_bq, L0_attn_Wk, L0_attn_bk,
           L0_attn_Wv, L0_attn_bv, L0_attn_Wo, L0_attn_bo,
           L0_ffn_w_gate, L0_ffn_w_up, L0_ffn_w_down, _dbg=None, _trace=False):
    x = np.ascontiguousarray(np.asarray(L0_attn_x, dtype=np.float32))
    Wq = np.asarray(L0_attn_Wq, dtype=np.float32)
    Wk = np.asarray(L0_attn_Wk, dtype=np.float32)
    Wv = np.asarray(L0_attn_Wv, dtype=np.float32)
    Wo = np.asarray(L0_attn_Wo, dtype=np.float32)
    wg = np.asarray(L0_ffn_w_gate, dtype=np.int32)
    wu = np.asarray(L0_ffn_w_up, dtype=np.int32)
    wd = np.asarray(L0_ffn_w_down, dtype=np.int32)

    def attw(Wm):
        return np.ascontiguousarray(
            Wm.reshape(DC, 128, DC, 128).transpose(2, 3, 0, 1).reshape(DC, 128, D))

    wqL, wkL, woL = attw(Wq), attw(Wk), attw(Wo)
    wvL = np.ascontiguousarray(Wv.T.reshape(DC, 128, D))
    wgL = wg.reshape(NH, 128, DC, 128).transpose(0, 3, 2, 1).reshape(NH, 128, D)
    wuL = wu.reshape(NH, 128, DC, 128).transpose(0, 3, 2, 1).reshape(NH, 128, D)
    wguL = np.ascontiguousarray(np.concatenate([wgL, wuL], axis=2))
    wdL = np.ascontiguousarray(wd.reshape(DC, 128, NH, 128).transpose(2, 3, 0, 1).reshape(NH, 128, D))
    biases = np.ascontiguousarray(np.stack([
        np.asarray(L0_attn_bk, dtype=np.float32),
        np.asarray(L0_attn_bv, dtype=np.float32),
        np.asarray(L0_attn_bq, dtype=np.float32),
        np.asarray(L0_attn_bo, dtype=np.float32)]))
    ident = np.eye(128, dtype=np.float32)
    permAB = _perm_mats()
    fconsts = np.zeros((1, 16), dtype=np.float32)
    fconsts[0, 0] = EPS
    fconsts[0, 1] = WS

    nc = build_program(dbg=_dbg)
    in_maps = []
    for core in range(NCORES := 8):
        b, half = core // 2, core % 2
        caq_t, cak_t = _host_tables(core)
        in_maps.append({
            "x_in": np.ascontiguousarray(x[b, half * T:(half + 1) * T, :]),
            "wqL": wqL, "wkL": wkL, "wvL": wvL, "woL": woL, "biases": biases,
            "wguL": wguL, "wdL": wdL,
            "caq": caq_t, "cak": cak_t, "ident": ident, "permAB": permAB,
            "fconsts": fconsts,
        })
    res = run_bass_kernel_spmd(nc, in_maps, core_ids=list(range(8)), trace=_trace)
    global LAST_RES
    LAST_RES = res
    if _dbg is not None:
        return res
    y = np.empty((B, S, D), dtype=np.float32)
    for core in range(8):
        b, half = core // 2, core % 2
        y[b, half * T:(half + 1) * T, :] = res.results[core]["y_out"]
    return y


# version 18
# speedup vs baseline: 1.0495x; 1.0495x over previous
import math

import numpy as np
import ml_dtypes

import concourse.bass as bass
import concourse.mybir as mybir
from concourse.bass_utils import run_bass_kernel_spmd

F32 = mybir.dt.float32
BF16 = mybir.dt.bfloat16
I32 = mybir.dt.int32
AF = mybir.ActivationFunctionType
ALU = mybir.AluOpType

B, S, D, H, HD = 4, 4096, 1024, 16, 64
T = 2048
DC = 8
FFH = 3072
NH = FFH // 128
TG = 1024
NTG = T // TG
QT = 512
NQC = T // QT
KC = S // 128
NPAIR = H // 2
EPS = 1e-5
WS = 1.0 / 127.0
KB = 1024
NTT = T // 128
LAST_RES = None


def build_program(dbg=None):
    M = {}
    _emit(bass.Bass("TRN2", target_bir_lowering=False, debug=False), M, True, dbg)
    nc = bass.Bass("TRN2", target_bir_lowering=False, debug=False)
    _emit(nc, M, False, dbg)
    return nc


def _emit(nc, M, dry, dbg=None):

    dt = nc.dram_tensor
    x_in = dt("x_in", [T, D], F32, kind="ExternalInput").ap()
    wqL = dt("wqL", [DC, 128, D], F32, kind="ExternalInput").ap()
    wkL = dt("wkL", [DC, 128, D], F32, kind="ExternalInput").ap()
    wvL = dt("wvL", [DC, 128, D], F32, kind="ExternalInput").ap()
    woL = dt("woL", [DC, 128, D], F32, kind="ExternalInput").ap()
    biases = dt("biases", [4, D], F32, kind="ExternalInput").ap()
    wguL = dt("wguL", [NH, 128, 2 * D], I32, kind="ExternalInput").ap()
    wdL = dt("wdL", [NH, 128, D], I32, kind="ExternalInput").ap()
    caq = dt("caq", [128, 2 * T], F32, kind="ExternalInput").ap()
    cak = dt("cak", [128, 2 * T], F32, kind="ExternalInput").ap()
    ident = dt("ident", [128, 128], F32, kind="ExternalInput").ap()
    permAB = dt("permAB", [128, 256], BF16, kind="ExternalInput").ap()
    fconsts = dt("fconsts", [1, 16], F32, kind="ExternalInput").ap()

    y_out = dt("y_out", [T, D], F32, kind="ExternalOutput").ap()
    dbg_out = dt("dbg_out", [128, 32768], F32, kind="ExternalOutput").ap() if dbg else None
    dbg_out2 = dt("dbg_out2", [128, 32768], F32, kind="ExternalOutput").ap() if dbg else None

    x1sp = dt("x1sp", [128, DC * T], F32)
    x2sp = dt("x2sp", [128, DC * T], F32)
    v_snd = dt("v_snd", [128, 16384], BF16)
    v_rcv = dt("v_rcv", [256, 16384], BF16)
    k_snd = dt("k_snd", [128, 16384], BF16)
    k_rcv = dt("k_rcv", [256, 16384], BF16)
    rbd = dt("rbd", [2, QT], F32)

    arena = nc.alloc_sbuf_tensor("arena", [128, 52900], F32)
    A0 = nc.lookup_mls("arena").memorylocations[0].addr
    A0 = (A0 + 31) // 32 * 32

    def at(name, shape, dtyp, off):
        return nc.alloc_sbuf_tensor_at(name, shape, dtyp, offset=A0 + off)

    R0, R1, R2, R3 = 0, 64 * KB, 128 * KB, 192 * KB
    xs = at("xs", [128, DC * T], F32, R0)
    xln2 = at("xln2", [128, DC * T], BF16, R0)
    x2stg = at("x2stg", [128, DC * QT], F32, R0)
    nsq = at("nsq", [128, S], BF16, R0 + 16 * KB)
    attn_fm = at("attn_fm", [128, DC * QT], BF16, R0 + 16 * KB)
    pT = at("pT", [128, 2 * 2 * QT], BF16, R0 + 24 * KB)
    woS = at("woS", [128, 2 * D], BF16, R0 + 28 * KB)
    qrope = at("qrope", [128, DC * T], BF16, R0 + 32 * KB)
    kfull = at("kfull", [128, DC * S], BF16, R1)
    hbuf = at("hbuf", [128, NH * TG], BF16, R1)
    xlnf = at("xlnf", [128, DC * TG], BF16, R1 + 48 * KB)
    kstg = at("kstg", [128, DC * T], BF16, R1)
    wfull = at("wfull", [128, DC * D], BF16, R1 + 32 * KB)
    qraw = at("qraw", [128, T], BF16, R1 + 48 * KB)
    rtmp = at("rtmp", [128, T], F32, R1 + 52 * KB)
    vfull = at("vfull", [128, KC * KB], BF16, R2)
    vstg = at("vstg", [128, 16 * KB], BF16, R2)
    m2row = at("m2row", [1, T], F32, R2)
    rows = at("rows", [1, 2 * T], F32, R2 + 8 * KB)
    xbf = at("xbf", [128, 2 * T], BF16, R2 + 24 * KB)
    sqbf = at("sqbf", [128, 2 * T], BF16, R2 + 32 * KB)
    wgu_s = at("wgu_s", [128, 2 * 4 * D], BF16, R2 + 24 * KB)
    wd_s = at("wd_s", [128, 2 * 4 * D], BF16, R2 + 40 * KB)
    lntmp = at("lntmp", [128, 512], F32, R2 + 56 * KB)
    silu_g = at("silu_g", [128, 2 * TG], BF16, R2 + 58 * KB)
    stg = at("stg", [128, D], F32, R1 + 48 * KB)
    stg2 = at("stg2", [128, D], F32, R1 + 52 * KB)
    cab = at("cab", [128, 2 * T], F32, R2 + 32 * KB)
    cs_idf = at("cs_idf", [128, 128], F32, R3)
    cs_perm = at("cs_perm", [128, 256], BF16, R3 + 512)
    onesc_bf = at("onesc_bf", [128, 1], BF16, R3 + 1024)
    onesr_bf = at("onesr_bf", [1, T], BF16, R3 + 1056)
    onesf = at("onesf", [65, 128], F32, R3 + 1056 + 4 * KB)
    negs = at("negs", [1, 128], F32, R3 + 1568 + 4 * KB)
    biasb = at("biasb", [1, D], BF16, R3 + 2080 + 4 * KB)
    nrm = at("nrm", [1, 64], F32, R3 + 2080 + 6 * KB)
    negC = at("negC", [128, NPAIR], F32, R3 + 2336 + 6 * KB)
    dens = at("dens", [65, QT], F32, R3 + 2368 + 6 * KB)
    recips = at("recips", [65, QT], F32, R3 + 2368 + 8 * KB)
    rbs = at("rbs", [128, QT], F32, R3 + 2368 + 10 * KB)
    fc = at("fc", [1, 16], F32, R3 + 2368 + 12 * KB)

    for i_, v_ in enumerate((EPS, WS)):
        nc.const_aps.register(F32, v_, fc[0:1, i_:i_ + 1])

    parena = nc.alloc_psum_tensor("parena", [128, 4096], F32)

    def bank(b, n=512):
        return parena[:, b * 512:b * 512 + n]

    sems, cnt = {}, {}
    for s_ in ("ld", "lw", "sp", "cc", "pe", "ac", "dv", "gp", "ldc", "ldx",
               "lwg", "lwd", "lww", "lwb", "lwo", "cabq", "snd", "kin", "vin",
               "x2l", "x2m", "bcd", "x2s", "sty", "sty2", "ldy", "sp1", "dbg"):
        sems[s_] = nc.alloc_semaphore(s_)
        cnt[s_] = 0

    def inc(instr, name, by=1):
        instr.then_inc(sems[name], by)
        cnt[name] += by
        return cnt[name]

    def val(name):
        return cnt[name]

    def wait(eng, name, v):
        if v > 0:
            eng.wait_ge(sems[name], v)

    def chain(e, m):
        wait(e, "dv", inc(m, "dv"))

    def dgo(e, name, out, in_):
        wait(e, name, cnt[name])
        m = e.dma_start(out=out, in_=in_)
        m.then_inc(sems[name], 16)
        cnt[name] += 16
        return m

    def mark(key, name):
        if dry:
            M[key] = (name, cnt[name])
        else:
            assert M[key] == (name, cnt[name]), (key, M[key], (name, cnt[name]))

    def wt(eng, key):
        if key not in M:
            assert dry or dbg, "unknown mark %r in final pass" % (key,)
            return
        nm, v = M[key]
        wait(eng, nm, v)

    def xs_c(c, t0, t1):
        return xs[:, c * T + t0:c * T + t1]

    NQC_E = 1 if dbg in ("attn1", "attn1b", "attn2") else NQC
    NPAIR_E = 1 if dbg == "attn1" else NPAIR
    SKIP_OP = dbg in ("attn1", "attn1b")

    with nc.Block() as blk:

        def pro_sync(e):
            dgo(e, "ldc", cs_idf[:], ident[:, :])
            dgo(e, "ldc", cs_perm[:], permAB[:, :])
            dgo(e, "ldc", fc[:], fconsts[:, :])
            mark("consts", "ldc")
        blk.sync(pro_sync)

        def pro_dv(e):
            e.memset(onesc_bf[:], 1.0)
            e.memset(onesr_bf[:], 1.0)
            e.memset(onesf[:], 1.0)
            e.memset(negs[:], -0.5)
            e.memset(nrm[:], 0.0)
            inc(e.memset(parena[:, 5 * 512:6 * 512], 1.0), "dv")
            mark("minit", "dv")
        blk.vector(pro_dv)

        def pro_ac(e):
            wt(e, "consts")
        blk.scalar(pro_ac)

        def pro_dv2(e):
            wt(e, "consts")
        blk.vector(pro_dv2)

        def pro_pe(e):
            wt(e, "consts")
            wt(e, "minit")
            v = inc(e.matmul(parena[0:32, 0:32], cs_idf[:, 0:32], cs_idf[:, 0:32],
                             start=True, stop=True), "pe")
            wait(e, "pe", v)
        blk.tensor(pro_pe)

        def load_sync(e, par, fam):
            for tt in range(par, NTT, 2):
                sg = stg if tt % 2 == 0 else stg2
                if tt >= 2:
                    wt(e, ("ldT", tt - 2))
                dgo(e, fam, sg[:], x_in[tt * 128:(tt + 1) * 128, :])
                mark(("ldL", tt), fam)
        blk.sync(lambda e: load_sync(e, 0, "ldx"))
        blk.scalar(lambda e: load_sync(e, 1, "ldy"))

        def load_pe(e):
            for tt in range(NTT):
                sg = stg if tt % 2 == 0 else stg2
                wt(e, ("ldL", tt))
                m = None
                for c in range(DC):
                    if c == 0 and tt >= 1:
                        wt(e, ("ldE4", tt - 1))
                    if c == 4 and tt >= 1:
                        wt(e, ("ldE", tt - 1))
                    m = e.transpose(parena[:, c * 512:c * 512 + 128],
                                    sg[:, c * 128:(c + 1) * 128], cs_idf[:])
                    if c == 3:
                        inc(m, "pe")
                        mark(("ldT4", tt), "pe")
                inc(m, "pe")
                mark(("ldT", tt), "pe")
            mark("ldone_pe", "pe")
        blk.tensor(load_pe)

        def load_dv(e):
            for tt in range(NTT):
                wt(e, ("ldT4", tt))
                m = None
                for c in range(DC):
                    if c == 4:
                        wt(e, ("ldT", tt))
                    m = e.tensor_copy(xs_c(c, tt * 128, (tt + 1) * 128),
                                      parena[:, c * 512:c * 512 + 128])
                    if c == 3:
                        inc(m, "dv")
                        mark(("ldE4", tt), "dv")
                inc(m, "dv")
                mark(("ldE", tt), "dv")
            mark("ldone", "dv")
        blk.vector(load_dv)

        def emit_ln_stats(tag, ready_keys, chunk_keys=None):
            def dv_b(e):
                for k in ready_keys:
                    wt(e, k)
                for c in range(DC):
                    h = c % 2
                    if chunk_keys:
                        for k in chunk_keys(c):
                            wt(e, k)
                    if c >= 2:
                        wt(e, (tag, "mm", c - 2))
                    m = e.tensor_copy(xbf[:, h * T:(h + 1) * T], xs_c(c, 0, T))
                    inc(m, "dv")
                    mark((tag, "cast", c), "dv")
            blk.vector(dv_b)

            def ac_b(e):
                for c in range(DC):
                    h = c % 2
                    wt(e, (tag, "cast", c))
                    if c >= 2:
                        wt(e, (tag, "mm", c - 2))
                    m = e.activation(out=sqbf[:, h * T:(h + 1) * T],
                                     in_=xbf[:, h * T:(h + 1) * T], func=AF.Square)
                    inc(m, "ac")
                    mark((tag, "sq", c), "ac")
            blk.scalar(ac_b)

            def pe_b(e):
                for c in range(DC):
                    h = c % 2
                    wt(e, (tag, "sq", c))
                    m = None
                    for ts in range(4):
                        e.matmul(parena[0:1, ts * 512:(ts + 1) * 512], onesc_bf[:],
                                 xbf[:, h * T + ts * 512:h * T + (ts + 1) * 512],
                                 start=(c == 0), stop=(c == DC - 1), tile_position=(0, 0))
                        m = e.matmul(parena[0:1, (4 + ts) * 512:(5 + ts) * 512], onesc_bf[:],
                                     sqbf[:, h * T + ts * 512:h * T + (ts + 1) * 512],
                                     start=(c == 0), stop=(c == DC - 1), tile_position=(0, 0))
                    inc(m, "pe")
                    mark((tag, "mm", c), "pe")
            blk.tensor(pe_b)

            def ac2_b(e):
                wt(e, (tag, "mm", DC - 1))
                m = e.copy(out=rows[0:1, :], in_=parena[0:1, 0:4096])
                inc(m, "ac")
                mark((tag, "statc"), "ac")
            blk.scalar(ac2_b)

            def dv2_b(e):
                wt(e, (tag, "statc"))
                chain(e, e.tensor_scalar(out=rows[0:1, :], in0=rows[0:1, :],
                                         scalar1=1.0 / D, scalar2=None, op0=ALU.mult))
                chain(e, e.tensor_tensor(out=m2row[0:1, :], in0=rows[0:1, 0:T],
                                         in1=rows[0:1, 0:T], op=ALU.mult))
                m = e.tensor_tensor(out=rows[0:1, T:2 * T], in0=rows[0:1, T:2 * T],
                                    in1=m2row[0:1, :], op=ALU.subtract)
                inc(m, "dv")
                mark((tag, "var"), "dv")
            blk.vector(dv2_b)

            def ac3_b(e):
                wt(e, (tag, "var"))
                m = e.activation(out=rows[0:1, T:2 * T], in_=rows[0:1, T:2 * T],
                                 func=AF.Sqrt, bias=EPS)
                inc(m, "ac")
                mark((tag, "sdev"), "ac")
            blk.scalar(ac3_b)

            def dv3_b(e):
                wt(e, (tag, "sdev"))
                m = e.reciprocal(rows[0:1, T:2 * T], rows[0:1, T:2 * T])
                inc(m, "dv")
                mark((tag, "rstd"), "dv")
            blk.vector(dv3_b)

        def emit_ln_mat(tag, ln, dst, dstw, t0, t1, pre_keys=()):
            nts = (t1 - t0) // 512

            def pe_b(e):
                wt(e, (ln, "rstd"))
                for k in pre_keys:
                    wt(e, k)
                m = None
                for ts in range(nts):
                    e.matmul(bank(ts), onesf[0:1, :],
                             rows[0:1, t0 + ts * 512:t0 + (ts + 1) * 512],
                             start=True, stop=True, tile_position=(0, 0))
                    m = e.matmul(bank(nts + ts), onesf[0:1, :],
                                 rows[0:1, T + t0 + ts * 512:T + t0 + (ts + 1) * 512],
                                 start=True, stop=True, tile_position=(0, 0))
                inc(m, "pe")
                mark((tag, "bc"), "pe")
            blk.tensor(pe_b)

            def dv_b(e):
                wt(e, (tag, "bc"))
                for c in range(DC):
                    m = None
                    for ts in range(nts):
                        chain(e, e.tensor_tensor(
                            out=lntmp[:, :],
                            in0=xs_c(c, t0 + ts * 512, t0 + (ts + 1) * 512),
                            in1=bank(ts), op=ALU.subtract))
                        m = e.tensor_tensor(
                            out=dst[:, c * dstw + ts * 512:c * dstw + (ts + 1) * 512],
                            in0=lntmp[:, :], in1=bank(nts + ts), op=ALU.mult)
                        chain(e, m)
                    mark((tag, "mat", c), "dv")
                mark((tag, "matdone"), "dv")
            blk.vector(dv_b)

        def emit_ffn(ph, gate_keys=()):
            ln = "ln%d" % ph
            for g in range(NTG):
                t0 = g * TG
                ftag = "f%d%d" % (ph, g)
                mtag = ftag + "m"
                prevf = "f%d%d" % (ph, g - 1) if g > 0 else \
                        ("f1%d" % (NTG - 1) if ph == 3 else None)
                pre = []
                if g > 0:
                    pre = [("f%d%d" % (ph, g - 1), "res", 1, DC - 1)]
                emit_ln_mat(mtag, ln, xlnf, TG, t0, t0 + TG, pre_keys=pre)

                def gp_b(e, ftag=ftag, prevf=prevf, g=g, ph=ph):
                    for k_ in gate_keys if g == 0 else ():
                        wt(e, k_)
                    if g == 0:
                        wt(e, (ln, "statc"))
                    for p in range(NH // 2):
                        if p >= 2:
                            wt(e, (ftag, "gu", 2 * p - 3))
                        elif prevf is not None:
                            wt(e, (prevf, "gu", NH - 4 + 2 * p + 1))
                        dgo(e, "lwg", wgu_s[:, (p % 2) * 4 * D:(p % 2 + 1) * 4 * D],
                            wguL[2 * p:2 * p + 2, :, :].rearrange("a p e -> p a e"))
                        mark((ftag, "wgu", p), "lwg")
                    for ts in range(2):
                        for p in range(NH // 4):
                            if p >= 2:
                                wt(e, (ftag, "dn", ts, 4 * p - 5))
                            elif ts == 1:
                                wt(e, (ftag, "dn", 0, NH - 8 + 4 * p + 3))
                            elif prevf is not None:
                                wt(e, (prevf, "dn", 1, NH - 8 + 4 * p + 3))
                            dgo(e, "lwd", wd_s[:, (p % 2) * 4 * D:(p % 2 + 1) * 4 * D],
                                wdL[4 * p:4 * p + 4, :, :].rearrange("a p e -> p a e"))
                            mark((ftag, "wd", ts, p), "lwd")
                blk.gpsimd(gp_b)

                def pe_b(e, ftag=ftag, mtag=mtag):
                    for k in range(NH):
                        wt(e, (ftag, "wgu", k // 2))
                        if k == 0:
                            wt(e, (mtag, "matdone"))
                        if k >= 2:
                            wt(e, (ftag, "silu", k - 2))
                            wt(e, (ftag, "hmul", k - 2))
                        for c in range(DC):
                            for ts in range(2):
                                m = e.matmul(
                                    bank(4 * (k % 2) + ts),
                                    wgu_s[:, ((k // 2) % 2) * 4 * D + (k % 2) * 2 * D + c * 128:
                                          ((k // 2) % 2) * 4 * D + (k % 2) * 2 * D + (c + 1) * 128],
                                    xlnf[:, c * TG + ts * 512:c * TG + (ts + 1) * 512],
                                    start=(c == 0), stop=(c == DC - 1))
                        inc(m, "pe")
                        mark((ftag, "gmm", k), "pe")
                        for c in range(DC):
                            for ts in range(2):
                                m = e.matmul(
                                    bank(4 * (k % 2) + 2 + ts),
                                    wgu_s[:, ((k // 2) % 2) * 4 * D + (k % 2) * 2 * D + D + c * 128:
                                          ((k // 2) % 2) * 4 * D + (k % 2) * 2 * D + D + (c + 1) * 128],
                                    xlnf[:, c * TG + ts * 512:c * TG + (ts + 1) * 512],
                                    start=(c == 0), stop=(c == DC - 1))
                        inc(m, "pe")
                        mark((ftag, "gu", k), "pe")
                    for ts in range(2):
                        for k in range(NH):
                            wt(e, (ftag, "wd", ts, k // 4))
                            if ts == 0 and k == 0:
                                wt(e, (ftag, "hmul", NH - 1))
                                wt(e, (ftag, "hmul", NH - 2))
                            if ts == 1 and k == 0:
                                for dd in range(DC):
                                    wt(e, (ftag, "res", 0, dd))
                            for dout in range(DC):
                                m = e.matmul(
                                    bank(dout),
                                    wd_s[:, ((k // 4) % 2) * 4 * D + (k % 4) * D + dout * 128:
                                          ((k // 4) % 2) * 4 * D + (k % 4) * D + (dout + 1) * 128],
                                    hbuf[:, k * TG + ts * 512:k * TG + (ts + 1) * 512],
                                    start=(k == 0), stop=(k == NH - 1))
                            inc(m, "pe")
                            mark((ftag, "dn", ts, k), "pe")
                blk.tensor(pe_b)

                def ac_b(e, ftag=ftag):
                    for k in range(NH):
                        wt(e, (ftag, "gmm", k))
                        if k >= 2:
                            wt(e, (ftag, "hmul", k - 2))
                        m = e.activation(out=silu_g[:, (k % 2) * TG:(k % 2 + 1) * TG],
                                         in_=bank(4 * (k % 2), n=1024), func=AF.Silu,
                                         scale=WS)
                        inc(m, "ac")
                        mark((ftag, "silu", k), "ac")
                blk.scalar(ac_b)

                def dv_b(e, ftag=ftag, t0=t0):
                    for k in range(NH):
                        wt(e, (ftag, "silu", k))
                        wt(e, (ftag, "gu", k))
                        m = e.tensor_tensor(out=hbuf[:, k * TG:(k + 1) * TG],
                                            in0=silu_g[:, (k % 2) * TG:(k % 2 + 1) * TG],
                                            in1=bank(4 * (k % 2) + 2, n=1024), op=ALU.mult)
                        inc(m, "dv")
                        mark((ftag, "hmul", k), "dv")
                    for ts in range(2):
                        wt(e, (ftag, "dn", ts, NH - 1))
                        for dout in range(DC):
                            m = e.scalar_tensor_tensor(
                                out=xs_c(dout, t0 + ts * 512, t0 + (ts + 1) * 512),
                                in0=bank(dout), scalar=0.5 * WS * WS,
                                in1=xs_c(dout, t0 + ts * 512, t0 + (ts + 1) * 512),
                                op0=ALU.mult, op1=ALU.add)
                            inc(m, "dv")
                            mark((ftag, "res", ts, dout), "dv")
                blk.vector(dv_b)
            mark(("ffn%d" % ph, "dv"), "dv")
            mark(("ffn%d" % ph, "pe"), "pe")

        emit_ln_stats("ln1", ["ldone"])
        emit_ffn(1)

        if dbg == "ffn1":
            def dump1(e):
                wt(e, ("ffn1", "dv"))
                dgo(e, "dbg", dbg_out[:, 0:DC * T], xs[:])
                dgo(e, "dbg", dbg_out2[0:1, 0:2 * T], rows[:, :])
                dgo(e, "dbg", dbg_out2[0:1, 2 * T:3 * T], m2row[:, :])
                wait(e, "dbg", val("dbg"))
            blk.sync(dump1)
            return nc

        emit_ln_stats("ln2", [("ffn1", "dv")])

        def spill_x1(e):
            wt(e, ("ln2", "cast", DC - 1))
            wt(e, ("ffn1", "dv"))
            dgo(e, "sp1", x1sp[:, :], xs[:])
            mark("x1spilled", "sp1")
        blk.sync(spill_x1)

        def spgate(e):
            wt(e, "x1spilled")
            wait(e, "dv", val("dv"))
            inc(e.nop(), "dv")
            mark("spgate", "dv")
        blk.vector(spgate)

        emit_ln_mat("ln2m", "ln2", xln2, T, 0, T, pre_keys=["spgate"])

        if dbg == "xln2":
            def dumpx(e):
                wt(e, ("ln2m", "matdone"))
                dgo(e, "dbg", dbg_out[:, 0:DC * T], xln2[:])
                e.wait_ge(sems["dbg"], cnt["dbg"])
            blk.gpsimd(dumpx)
            return

        def pw_gp(e, rng):
            if 0 in rng:
                wt(e, ("ffn1", "pe"))
            for i in rng:
                wl = (wkL, wvL, wqL)[i]
                if i > 0:
                    wt(e, ("pj", "wfree", i - 1))
                dgo(e, "lww", wfull[:, :], wl[:, :, :].rearrange("c p e -> p c e"))
                mark(("pj", "w", i), "lww")
                if i > 0:
                    wt(e, ("pj", "bfree", i - 1))
                dgo(e, "lwb", biasb[:, :], biases[i:i + 1, :])
                mark(("pj", "b", i), "lwb")
        blk.gpsimd(lambda e: pw_gp(e, (0, 1)))

        def pwo_gp(e):
            wt(e, ("pj", "bfree", 2))
            dgo(e, "lwb", biasb[:, :], biases[3:4, :])
            mark(("pj", "b", 3), "lwb")

        def wo_gp(e):
            wt(e, ("qp", "ropedone"))

            def wo_load(qt, dout):
                idx = qt * DC + dout
                if idx >= 2:
                    wt(e, ("opmm", (idx - 2) // DC, (idx - 2) % DC))
                dgo(e, "lwo", woS[:, (idx % 2) * D:(idx % 2 + 1) * D],
                    woL[:, :, dout * 128:(dout + 1) * 128].rearrange("c p j -> p c j"))
                mark(("wo", qt, dout), "lwo")

            if not SKIP_OP:
                wo_load(0, 0)
                wo_load(0, 1)
            for qt in range(NQC_E):
                if not SKIP_OP:
                    for dout in range(2, DC):
                        wo_load(qt, dout)
                    if qt + 1 < NQC_E:
                        wo_load(qt + 1, 0)
                        wo_load(qt + 1, 1)

        def pw_sync(e):
            wt(e, ("ln2", "statc"))
            dgo(e, "cabq", cab[:, :], cak[:, :])
            mark("cab_k", "cabq")
        blk.sync(pw_sync)

        def emit_proj_fm(idx, tagp, dst, prev_done_key, cab_key):
            def pe_b(e):
                wt(e, ("pj", "w", idx))
                wt(e, ("pj", "b", idx))
                wt(e, prev_done_key)
                for pc in range(DC):
                    if pc >= 1:
                        wt(e, (tagp, "qev", pc - 1))
                    for c in range(DC):
                        for ts in range(4):
                            e.matmul(bank(ts),
                                     wfull[:, c * D + pc * 128:c * D + (pc + 1) * 128],
                                     xln2[:, c * T + ts * 512:c * T + (ts + 1) * 512],
                                     start=(c == 0), stop=False)
                    m = None
                    for ts in range(4):
                        m = e.matmul(bank(ts), biasb[:, pc * 128:(pc + 1) * 128],
                                     onesr_bf[0:1, ts * 512:(ts + 1) * 512],
                                     start=False, stop=True)
                    inc(m, "pe")
                    mark((tagp, "qmm", pc), "pe")
                    wt(e, (tagp, "qev", pc))
                    for th in range(2):
                        if pc >= 1 or th >= 1:
                            prev = (tagp, "rope", pc, th - 1) if th >= 1 else (tagp, "rope", pc - 1, 1)
                            wt(e, prev)
                        for hh in range(2):
                            e.matmul(bank(4 + hh), cs_perm[:, 0:128],
                                     qraw[:, th * KB + hh * 512:th * KB + (hh + 1) * 512],
                                     start=True, stop=True)
                            m = e.matmul(bank(6 + hh), cs_perm[:, 128:256],
                                         qraw[:, th * KB + hh * 512:th * KB + (hh + 1) * 512],
                                         start=True, stop=True)
                        inc(m, "pe")
                        mark((tagp, "perm", pc, th), "pe")
                mark(("pj", "wfree", idx), "pe")
                mark(("pj", "bfree", idx), "pe")
            blk.tensor(pe_b)

            def ac_b(e):
                for pc in range(DC):
                    wt(e, (tagp, "qmm", pc))
                    if pc >= 1:
                        wt(e, (tagp, "perm", pc - 1, 1))
                    m = e.activation(out=qraw[:, :], in_=parena[:, 0:2048], func=AF.Copy)
                    inc(m, "ac")
                    mark((tagp, "qev", pc), "ac")
            blk.scalar(ac_b)

            def dv_b(e):
                wt(e, cab_key)
                for pc in range(DC):
                    for th in range(2):
                        wt(e, (tagp, "perm", pc, th))
                        m = None
                        for hh in range(2):
                            sl = th * KB + hh * 512
                            e.tensor_tensor(out=rtmp[:, 0:512], in0=bank(4 + hh),
                                            in1=cab[:, sl:sl + 512], op=ALU.mult)
                            chain(e, e.tensor_tensor(out=rtmp[:, 512:1024], in0=bank(6 + hh),
                                                     in1=cab[:, T + sl:T + sl + 512],
                                                     op=ALU.mult))
                            m = e.tensor_tensor(out=dst[:, pc * T + sl:pc * T + sl + 512],
                                                in0=rtmp[:, 0:512], in1=rtmp[:, 512:1024],
                                                op=ALU.add)
                            chain(e, m)
                        mark((tagp, "rope", pc, th), "dv")
                mark((tagp, "ropedone"), "dv")
            blk.vector(dv_b)

        emit_proj_fm(0, "kp", kstg, ("ln2m", "matdone"), "cab_k")

        def cabq_sync(e):
            wt(e, ("kp", "ropedone"))
            dgo(e, "cabq", cab[:, :], caq[:, :])
            mark("cab_q", "cabq")
        blk.sync(cabq_sync)

        def vproj_pe(e):
            wt(e, ("pj", "w", 1))
            wt(e, ("pj", "b", 1))
            wt(e, ("kp", "ropedone"))
            for tt in range(NTT):
                bs = 4 + 2 * (tt % 2)
                if tt >= 2:
                    wt(e, ("pj", "vev", tt - 2))
                for c in range(DC):
                    for eh in range(2):
                        e.matmul(bank(bs + eh),
                                 xln2[:, c * T + tt * 128:c * T + (tt + 1) * 128],
                                 wfull[:, c * D + eh * 512:c * D + (eh + 1) * 512],
                                 start=(c == 0), stop=False)
                for eh in range(2):
                    m = e.matmul(bank(bs + eh), onesr_bf[0:1, 0:128],
                                 biasb[:, eh * 512:(eh + 1) * 512], start=False, stop=True)
                inc(m, "pe")
                mark(("pj", "vmm", tt), "pe")
            mark(("pj", "wfree", 1), "pe")
            mark(("pj", "bfree", 1), "pe")
        blk.tensor(vproj_pe)

        def vproj_ac(e):
            for tt in range(NTT):
                bs = 4 + 2 * (tt % 2)
                wt(e, ("pj", "vmm", tt))
                m = e.activation(out=vstg[:, tt * KB:(tt + 1) * KB],
                                 in_=parena[:, bs * 512:(bs + 2) * 512], func=AF.Copy)
                inc(m, "ac")
                mark(("pj", "vev", tt), "ac")
            mark("vdone", "ac")
        blk.scalar(vproj_ac)

        emit_proj_fm(2, "qp", qrope, "vdone", "cab_q")

        if dbg == "qkv":
            def dumpq(e):
                wt(e, ("qp", "ropedone"))
                wt(e, "vdone")
                dgo(e, "dbg", dbg_out[:, 0:DC * T], qrope[:])
                dgo(e, "dbg", dbg_out[:, DC * T:2 * DC * T], kstg[:])
                dgo(e, "dbg", dbg_out2[:, 0:16 * KB], vstg[:])
                e.wait_ge(sems["dbg"], cnt["dbg"])
            blk.gpsimd(dumpq)
            return

        def ag_sync(e):
            wt(e, ("kp", "ropedone"))
            dgo(e, "snd", k_snd[:, :], kstg[:, :])
            mark("kout", "snd")
            wt(e, "vdone")
            dgo(e, "snd", v_snd[:, :], vstg[:, :])
            mark("vout", "snd")
        blk.sync(ag_sync)

        def agK_gp(e):
            wt(e, "kout")
            m = e.collective_compute(
                "AllGather", ALU.bypass,
                replica_groups=[[0, 1], [2, 3], [4, 5], [6, 7]],
                ins=[k_snd.ap().opt()], outs=[k_rcv.ap().opt()])
            inc(m, "cc")
            mark("agK", "cc")
        blk.gpsimd(agK_gp)

        blk.gpsimd(lambda e: pw_gp(e, (2,)))

        def agV_gp(e):
            wt(e, "vout")
            m = e.collective_compute(
                "AllGather", ALU.bypass,
                replica_groups=[[0, 1], [2, 3], [4, 5], [6, 7]],
                ins=[v_snd.ap().opt()], outs=[v_rcv.ap().opt()])
            inc(m, "cc")
            mark("agV", "cc")
        blk.gpsimd(agV_gp)

        blk.gpsimd(pwo_gp)

        def ag_in_sync(e):
            wt(e, "agK")
            for c in range(4):
                for r in range(2):
                    dgo(e, "kin", kfull[:, c * S + r * T:c * S + r * T + T],
                        k_rcv[r * 128:(r + 1) * 128, c * T:(c + 1) * T])
                mark(("kin", c), "kin")
            wt(e, "agV")
            for tc in range(16):
                dgo(e, "vin", vfull[:, tc * KB:(tc + 1) * KB],
                    v_rcv[0:128, tc * KB:(tc + 1) * KB])
            wt(e, ("qp", "qmm", DC - 1))
            for c in range(4, 6):
                for r in range(2):
                    dgo(e, "kin", kfull[:, c * S + r * T:c * S + r * T + T],
                        k_rcv[r * 128:(r + 1) * 128, c * T:(c + 1) * T])
                mark(("kin", c), "kin")
            wt(e, ("qp", "ropedone"))
            for c in range(6, 8):
                for r in range(2):
                    dgo(e, "kin", kfull[:, c * S + r * T:c * S + r * T + T],
                        k_rcv[r * 128:(r + 1) * 128, c * T:(c + 1) * T])
                mark(("kin", c), "kin")
            mark("kvin", "kin")
            for tc in range(16):
                dgo(e, "vin", vfull[:, (16 + tc) * KB:(17 + tc) * KB],
                    v_rcv[128:256, tc * KB:(tc + 1) * KB])
            mark("vin", "vin")
        blk.sync(ag_in_sync)

        def nrm_ac(e):
            wt(e, ("qp", "ropedone"))
            for pc in range(DC):
                h = pc % 2
                if pc >= 2:
                    wt(e, ("nqm", pc - 2))
                m = e.activation(out=nsq[:, h * T:(h + 1) * T],
                                 in_=qrope[:, pc * T:(pc + 1) * T], func=AF.Square)
                inc(m, "ac")
                mark(("nq", pc), "ac")
            for pc in range(DC):
                wt(e, ("kin", pc))
                if pc == 0:
                    wt(e, ("nqm", DC - 1))
                else:
                    wt(e, ("nkm", pc - 1))
                m = e.activation(out=nsq[:, 0:S], in_=kfull[:, pc * S:(pc + 1) * S],
                                 func=AF.Square)
                inc(m, "ac")
                mark(("nk", pc), "ac")
        blk.scalar(nrm_ac)

        def nrm_pe(e):
            for pc in range(DC):
                h = pc % 2
                R = 32 * (pc // 2)
                wt(e, ("nq", pc))
                m = None
                for ts in range(4):
                    m = e.matmul(parena[R:R + 1, (h * 4 + ts) * 512:(h * 4 + ts + 1) * 512],
                                 onesc_bf[:], nsq[:, (h * 4 + ts) * 512:(h * 4 + ts + 1) * 512],
                                 start=True, stop=True, tile_position=(0, R))
                inc(m, "pe")
                mark(("nqm", pc), "pe")
            for pc in range(DC):
                R = 32 * (pc % 4)
                wt(e, ("nk", pc))
                if pc < 4:
                    wt(e, ("nqr", 2 * pc))
                    wt(e, ("nqr", 2 * pc + 1))
                else:
                    wt(e, ("nkr", pc - 4))
                m = None
                for ts in range(8):
                    m = e.matmul(parena[R:R + 1, ts * 512:(ts + 1) * 512],
                                 onesc_bf[:], nsq[:, ts * 512:(ts + 1) * 512],
                                 start=True, stop=True, tile_position=(0, R))
                inc(m, "pe")
                mark(("nkm", pc), "pe")
            wt(e, ("nsum",))
            m = e.matmul(parena[:, 0:8], negs[:, :], nrm[0:1, 32:40],
                         start=True, stop=True)
            inc(m, "pe")
            mark("negCm", "pe")
        blk.tensor(nrm_pe)

        def nrm_red(e, par, sem):
            ax = mybir.AxisListType.X
            for pc in range(par, DC, 1 if par == 0 else 9):
                h = pc % 2
                R = 32 * (pc // 2)
                wt(e, ("nqm", pc))
                m = e.tensor_reduce(out=nrm[0:1, pc:pc + 1],
                                    in_=parena[R:R + 1, h * 2048:(h + 1) * 2048],
                                    axis=ax, op=ALU.max)
                wait(e, sem, inc(m, sem))
                mark(("nqr", pc), sem)
            for pc in range(par, DC, 1 if par == 0 else 9):
                R = 32 * (pc % 4)
                wt(e, ("nkm", pc))
                m = e.tensor_reduce(out=nrm[0:1, 16 + pc:17 + pc],
                                    in_=parena[R:R + 1, 0:4096],
                                    axis=ax, op=ALU.max)
                wait(e, sem, inc(m, sem))
                mark(("nkr", pc), sem)
        blk.vector(lambda e: nrm_red(e, 0, "dv"))

        def nrm_gp(e):
            wt(e, ("nkr", DC - 1))
            m = e.tensor_tensor(out=nrm[0:1, 32:48], in0=nrm[0:1, 0:16],
                                in1=nrm[0:1, 16:32], op=ALU.add)
            inc(m, "gp")
            mark(("nsum",), "gp")
        blk.gpsimd(nrm_gp)

        def nrm_ac2(e):
            wt(e, "negCm")
            m = e.copy(out=negC[:, :], in_=parena[:, 0:8])
            inc(m, "ac")
            mark("negCr", "ac")
        blk.scalar(nrm_ac2)

        blk.gpsimd(wo_gp)

        if dbg == "kv":
            def dumpkv(e):
                wt(e, "kvin")
                wt(e, "vin")
                wt(e, "negCr")
                dgo(e, "dbg", dbg_out[:, :], kfull[:])
                dgo(e, "dbg", dbg_out2[:, 0:KC * KB], vfull[:])
                dgo(e, "dbg", dbg_out2[:, KC * KB - NPAIR:KC * KB], negC[:])
                dgo(e, "dbg", dbg_out2[0:1, KC * KB - 72:KC * KB - 8], nrm[:])
                e.wait_ge(sems["dbg"], cnt["dbg"])
            blk.gpsimd(dumpkv)
            return

        OPB = 7

        def attn_pe(e):
            wt(e, "kvin")
            wt(e, "vin")
            wt(e, "negCr")

            def emit_outproj(e, qt, dout):
                wt(e, ("pj", "b", 3))
                wt(e, ("epiTT", qt * NPAIR + NPAIR - 1))
                idx = qt * DC + dout
                wt(e, ("wo", qt, dout))
                if dout >= 1:
                    wt(e, ("opev", qt, dout - 1))
                for c in range(DC):
                    e.matmul(bank(OPB),
                             woS[:, (idx % 2) * D + c * 128:(idx % 2) * D + (c + 1) * 128],
                             attn_fm[:, c * QT:(c + 1) * QT],
                             start=(c == 0), stop=False)
                m = e.matmul(bank(OPB), biasb[:, dout * 128:(dout + 1) * 128],
                             onesr_bf[0:1, 0:QT], start=False, stop=True)
                inc(m, "pe")
                mark(("opmm", qt, dout), "pe")

            for qt in range(NQC_E):
                for pr in range(NPAIR_E):
                    it0 = (qt * NPAIR + pr)
                    pvb = 4 + 2 * (pr % 2)
                    dnb = pvb + 1
                    for kc in range(KC + 1):
                        if kc < KC:
                            sb = 2 * (kc % 2)
                            if kc >= 2:
                                wt(e, ("exp", it0, kc - 2))
                            elif it0 > 0 and kc == 0:
                                wt(e, ("exp", it0 - 1, KC - 2))
                            elif it0 > 0:
                                wt(e, ("exp", it0 - 1, KC - 1))
                            if kc == 0:
                                if it0 >= 2:
                                    wt(e, ("epiTT", it0 - 2))
                                if pr == 1 and qt > 0 and not SKIP_OP:
                                    wt(e, ("opev", qt - 1, DC - 1))
                            e.matmul(parena[:, sb * 512:sb * 512 + 512],
                                     kfull[0:64, pr * S + kc * 128:pr * S + (kc + 1) * 128],
                                     qrope[0:64, pr * T + qt * QT:pr * T + (qt + 1) * QT],
                                     start=True, stop=True, tile_position=(0, 0))
                            m = e.matmul(parena[:, sb * 512 + 512:sb * 512 + 1024],
                                         kfull[64:128, pr * S + kc * 128:pr * S + (kc + 1) * 128],
                                         qrope[64:128, pr * T + qt * QT:pr * T + (qt + 1) * QT],
                                         start=True, stop=True, tile_position=(64, 0))
                            inc(m, "pe")
                            mark(("qk", it0, kc), "pe")
                        if kc >= 1:
                            kp = kc - 1
                            pb = (kp % 2) * KB
                            wt(e, ("exp", it0, kp))
                            e.matmul(parena[0:64, pvb * 512:pvb * 512 + 512],
                                     vfull[:, kp * KB + pr * 128:kp * KB + pr * 128 + 64],
                                     pT[:, pb:pb + 512],
                                     start=(kp == 0), stop=(kp == KC - 1),
                                     tile_position=(0, 0))
                            e.matmul(parena[64:128, pvb * 512:pvb * 512 + 512],
                                     vfull[:, kp * KB + pr * 128 + 64:kp * KB + (pr + 1) * 128],
                                     pT[:, pb + 512:pb + 1024],
                                     start=(kp == 0), stop=(kp == KC - 1),
                                     tile_position=(0, 64))
                            e.matmul(parena[0:1, dnb * 512:dnb * 512 + 512], onesc_bf[:],
                                     pT[:, pb:pb + 512],
                                     start=(kp == 0), stop=(kp == KC - 1),
                                     tile_position=(0, 0))
                            m = e.matmul(parena[64:65, dnb * 512:dnb * 512 + 512], onesc_bf[:],
                                         pT[:, pb + 512:pb + 1024],
                                         start=(kp == 0), stop=(kp == KC - 1),
                                         tile_position=(0, 64))
                            inc(m, "pe")
                            mark(("pv", it0, kp), "pe")
                        if (not SKIP_OP and pr == 0 and qt > 0 and kc >= 4
                                and kc % 2 == 0 and (kc - 4) // 2 < DC):
                            emit_outproj(e, qt - 1, (kc - 4) // 2)
                if SKIP_OP or qt != NQC_E - 1:
                    continue
                for dout in range(DC):
                    emit_outproj(e, qt, dout)
            mark("attn_pe", "pe")
        blk.tensor(attn_pe)

        def attn_ac(e):
            for qt in range(NQC_E):
                for pr in range(NPAIR_E):
                    it0 = (qt * NPAIR + pr)
                    for kc in range(KC):
                        sb = 2 * (kc % 2)
                        pb = (kc % 2) * KB
                        wt(e, ("qk", it0, kc))
                        if kc >= 2:
                            wt(e, ("pv", it0, kc - 2))
                        m = e.activation(out=pT[:, pb:pb + KB],
                                         in_=parena[:, sb * 512:sb * 512 + KB],
                                         func=AF.Exp, bias=negC[:, pr:pr + 1], scale=1.0)
                        inc(m, "ac")
                        mark(("exp", it0, kc), "ac")

        blk.scalar(attn_ac)

        def attn_dv(e):
            for qt in range(NQC_E):
                for pr in range(NPAIR_E):
                    it0 = (qt * NPAIR + pr)
                    wt(e, ("pv", it0, KC - 1))
                    if it0 > 0:
                        wt(e, ("bc", it0 - 1))
                    dnb = 5 + 2 * (pr % 2)
                    m = e.tensor_copy(dens[:, :], parena[0:65, dnb * 512:(dnb + 1) * 512])
                    inc(m, "dv")
                    mark(("denc", it0), "dv")
                    wait(e, "dv", val("dv"))
                    m = e.reciprocal(recips[:, :], dens[:, :])
                    inc(m, "dv")
                    mark(("recip", it0), "dv")
                    wt(e, ("bc", it0))
                    if pr == 0 and qt > 0:
                        wt(e, ("opmm", qt - 1, DC - 1))
                    pvb = 4 + 2 * (pr % 2)
                    m = e.tensor_tensor(out=attn_fm[:, pr * QT:(pr + 1) * QT],
                                        in0=parena[:, pvb * 512:(pvb + 1) * 512],
                                        in1=rbs[:, :], op=ALU.mult)
                    inc(m, "dv")
                    mark(("epiTT", it0), "dv")
                if SKIP_OP:
                    continue
                wt(e, ("x2ld", qt))
                for dout in range(DC):
                    wt(e, ("opmm", qt, dout))
                    m = e.tensor_tensor(out=x2stg[:, dout * QT:(dout + 1) * QT],
                                        in0=bank(OPB),
                                        in1=x2stg[:, dout * QT:(dout + 1) * QT],
                                        op=ALU.add)
                    inc(m, "dv")
                    mark(("opev", qt, dout), "dv")
                mark(("x2rdy", qt), "dv")
            mark("attn_dv", "dv")
        blk.vector(attn_dv)

        def attn_sync(e):
            wt(e, "x1spilled")
            wt(e, ("qp", "ropedone"))
            for qt in range(NQC_E):
                if not SKIP_OP:
                    if qt >= 1:
                        wt(e, ("x2st", qt - 1))
                    dgo(e, "x2l", x2stg[:, :],
                        x1sp[:, :].rearrange("p (c t) -> p c t", c=DC)[:, :, qt * QT:(qt + 1) * QT])
                    mark(("x2ld", qt), "x2l")
                for pr in range(NPAIR_E):
                    it0 = qt * NPAIR + pr
                    wt(e, ("recip", it0))
                    if it0 > 0:
                        wt(e, ("epiTT", it0 - 1))
                    dgo(e, "bcd", rbd[0:1, :], recips[0:1, :])
                    dgo(e, "bcd", rbd[1:2, :], recips[64:65, :])
                    dgo(e, "bcd", rbs[0:64, :], rbd[0:1, :].broadcast_to((64, QT)))
                    dgo(e, "bcd", rbs[64:128, :], rbd[1:2, :].broadcast_to((64, QT)))
                    mark(("bc", it0), "bcd")
                if not SKIP_OP:
                    wt(e, ("x2rdy", qt))
                    dgo(e, "x2s",
                        x2sp[:, :].rearrange("p (c t) -> p c t", c=DC)[:, :, qt * QT:(qt + 1) * QT],
                        x2stg[:, :])
                    mark(("x2st", qt), "x2s")
            mark("x2done", "x2s")
        blk.sync(attn_sync)

        if dbg in ("attn1", "attn1b"):
            def dumpa1(e):
                wt(e, ("epiTT", NPAIR_E - 1))
                dgo(e, "dbg", dbg_out[:, 0:DC * QT], attn_fm[:])
                dgo(e, "dbg", dbg_out2[0:65, 0:QT], dens[:])
                e.wait_ge(sems["dbg"], cnt["dbg"])
            blk.gpsimd(dumpa1)
            return

        if dbg == "attn2":
            def dumpa2(e):
                wt(e, ("x2st", 0))
                dgo(e, "dbg", dbg_out[:, 0:DC * QT], x2stg[:])
                e.wait_ge(sems["dbg"], cnt["dbg"])
            blk.gpsimd(dumpa2)
            return

        if dbg == "attn":
            def dumpa(e):
                wt(e, "x2done")
                dgo(e, "dbg", xs[:, :], x2sp[:, :])
                wait(e, "dbg", val("dbg"))
                dgo(e, "dbg", dbg_out[:, 0:DC * T], xs[:])
                wait(e, "dbg", val("dbg"))
            blk.sync(dumpa)
            return nc

        def x2_reload(e):
            wt(e, "x2done")
            dgo(e, "x2l", xs[:, 0:4 * T], x2sp[:, 0:4 * T])
            mark(("x2h", 0), "x2l")
        blk.sync(x2_reload)

        def x2_reload_hi(e):
            wt(e, "x2done")
            dgo(e, "x2m", xs[:, 4 * T:DC * T], x2sp[:, 4 * T:DC * T])
            mark(("x2h", 1), "x2m")
        blk.gpsimd(x2_reload_hi)

        def x2gate(e):
            wt(e, ("x2h", 0))
            wt(e, ("x2h", 1))
            wt(e, "attn_dv")
            inc(e.nop(), "dv")
            mark("x2ok", "dv")
        blk.vector(x2gate)

        emit_ln_stats("ln3", ["attn_dv"],
                      chunk_keys=lambda c: [("x2h", c // 4)])
        emit_ffn(3, gate_keys=["attn_pe", "attn_dv"])

        def out_pe(e):
            wt(e, ("ffn3", "dv"))
            for tt in range(NTT):
                m = None
                for c in range(DC):
                    if c == 0 and tt >= 1:
                        wt(e, ("oev4", tt - 1))
                    if c == 4 and tt >= 1:
                        wt(e, ("oev", tt - 1))
                    m = e.transpose(parena[:, c * 512:c * 512 + 128],
                                    xs_c(c, tt * 128, (tt + 1) * 128), cs_idf[:])
                    if c == 3:
                        inc(m, "pe")
                        mark(("oT4", tt), "pe")
                inc(m, "pe")
                mark(("oT", tt), "pe")
        blk.tensor(out_pe)

        def out_dv(e):
            for tt in range(NTT):
                sg = stg if tt % 2 == 0 else stg2
                if tt >= 2:
                    wt(e, ("ost", tt - 2))
                wt(e, ("oT4", tt))
                m = None
                for c in range(DC):
                    if c == 4:
                        wt(e, ("oT", tt))
                    m = e.tensor_copy(sg[:, c * 128:(c + 1) * 128],
                                      parena[:, c * 512:c * 512 + 128])
                    if c == 3:
                        inc(m, "dv")
                        mark(("oev4", tt), "dv")
                inc(m, "dv")
                mark(("oev", tt), "dv")
        blk.vector(out_dv)

        def out_sync(e, par, fam):
            for tt in range(par, NTT, 2):
                sg = stg if tt % 2 == 0 else stg2
                wt(e, ("oev", tt))
                dgo(e, fam, y_out[tt * 128:(tt + 1) * 128, :], sg[:])
                mark(("ost", tt), fam)
            wait(e, "sty", val("sty"))
            wait(e, "sty2", val("sty2"))
        blk.sync(lambda e: out_sync(e, 0, "sty"))
        blk.gpsimd(lambda e: out_sync(e, 1, "sty2"))

    return None


def _host_tables(core):
    inv = 10000.0 ** (-np.arange(32, dtype=np.float64) / 32.0)
    pos = (core % 2) * T + np.arange(T, dtype=np.float64)
    f = inv[:, None] * pos[None, :]
    cos, sin = np.cos(f), np.sin(f)
    ca64 = np.concatenate([cos, sin], axis=0)
    cb64 = np.concatenate([-sin, cos], axis=0)
    CA = np.concatenate([ca64, ca64], axis=0)
    CB = np.concatenate([cb64, cb64], axis=0)
    caq = np.concatenate([CA, CB], axis=1) * 0.125
    cak = np.concatenate([CA, CB], axis=1)
    return caq.astype(np.float32), cak.astype(np.float32)


def _perm_mats():
    pa = np.zeros((128, 128), dtype=np.float32)
    pb = np.zeros((128, 128), dtype=np.float32)
    for hp in range(2):
        for i in range(64):
            src = 64 * hp + 2 * (i % 32)
            pa[src, 64 * hp + i] = 1.0
            pb[src + 1, 64 * hp + i] = 1.0
    return np.concatenate([pa, pb], axis=1).astype(ml_dtypes.bfloat16)


def kernel(L0_attn_x, L0_attn_Wq, L0_attn_bq, L0_attn_Wk, L0_attn_bk,
           L0_attn_Wv, L0_attn_bv, L0_attn_Wo, L0_attn_bo,
           L0_ffn_w_gate, L0_ffn_w_up, L0_ffn_w_down, _dbg=None, _trace=False):
    x = np.ascontiguousarray(np.asarray(L0_attn_x, dtype=np.float32))
    Wq = np.asarray(L0_attn_Wq, dtype=np.float32)
    Wk = np.asarray(L0_attn_Wk, dtype=np.float32)
    Wv = np.asarray(L0_attn_Wv, dtype=np.float32)
    Wo = np.asarray(L0_attn_Wo, dtype=np.float32)
    wg = np.asarray(L0_ffn_w_gate, dtype=np.int32)
    wu = np.asarray(L0_ffn_w_up, dtype=np.int32)
    wd = np.asarray(L0_ffn_w_down, dtype=np.int32)

    def attw(Wm):
        return np.ascontiguousarray(
            Wm.reshape(DC, 128, DC, 128).transpose(2, 3, 0, 1).reshape(DC, 128, D))

    wqL, wkL, woL = attw(Wq), attw(Wk), attw(Wo)
    wvL = np.ascontiguousarray(Wv.T.reshape(DC, 128, D))
    wgL = wg.reshape(NH, 128, DC, 128).transpose(0, 3, 2, 1).reshape(NH, 128, D)
    wuL = wu.reshape(NH, 128, DC, 128).transpose(0, 3, 2, 1).reshape(NH, 128, D)
    wguL = np.ascontiguousarray(np.concatenate([wgL, wuL], axis=2))
    wdL = np.ascontiguousarray(wd.reshape(DC, 128, NH, 128).transpose(2, 3, 0, 1).reshape(NH, 128, D))
    biases = np.ascontiguousarray(np.stack([
        np.asarray(L0_attn_bk, dtype=np.float32),
        np.asarray(L0_attn_bv, dtype=np.float32),
        np.asarray(L0_attn_bq, dtype=np.float32),
        np.asarray(L0_attn_bo, dtype=np.float32)]))
    ident = np.eye(128, dtype=np.float32)
    permAB = _perm_mats()
    fconsts = np.zeros((1, 16), dtype=np.float32)
    fconsts[0, 0] = EPS
    fconsts[0, 1] = WS

    nc = build_program(dbg=_dbg)
    in_maps = []
    for core in range(NCORES := 8):
        b, half = core // 2, core % 2
        caq_t, cak_t = _host_tables(core)
        in_maps.append({
            "x_in": np.ascontiguousarray(x[b, half * T:(half + 1) * T, :]),
            "wqL": wqL, "wkL": wkL, "wvL": wvL, "woL": woL, "biases": biases,
            "wguL": wguL, "wdL": wdL,
            "caq": caq_t, "cak": cak_t, "ident": ident, "permAB": permAB,
            "fconsts": fconsts,
        })
    res = run_bass_kernel_spmd(nc, in_maps, core_ids=list(range(8)), trace=_trace)
    global LAST_RES
    LAST_RES = res
    if _dbg is not None:
        return res
    y = np.empty((B, S, D), dtype=np.float32)
    for core in range(8):
        b, half = core // 2, core % 2
        y[b, half * T:(half + 1) * T, :] = res.results[core]["y_out"]
    return y


# version 19
# speedup vs baseline: 1.0602x; 1.0102x over previous
import math

import numpy as np
import ml_dtypes

import concourse.bass as bass
import concourse.mybir as mybir
from concourse.bass_utils import run_bass_kernel_spmd

F32 = mybir.dt.float32
BF16 = mybir.dt.bfloat16
I32 = mybir.dt.int32
AF = mybir.ActivationFunctionType
ALU = mybir.AluOpType

B, S, D, H, HD = 4, 4096, 1024, 16, 64
T = 2048
DC = 8
FFH = 3072
NH = FFH // 128
TG = 1024
NTG = T // TG
QT = 512
NQC = T // QT
KC = S // 128
NPAIR = H // 2
EPS = 1e-5
WS = 1.0 / 127.0
KB = 1024
NTT = T // 128
LAST_RES = None


def build_program(dbg=None):
    M = {}
    _emit(bass.Bass("TRN2", target_bir_lowering=False, debug=False), M, True, dbg)
    nc = bass.Bass("TRN2", target_bir_lowering=False, debug=False)
    _emit(nc, M, False, dbg)
    return nc


def _emit(nc, M, dry, dbg=None):

    dt = nc.dram_tensor
    x_in = dt("x_in", [T, D], F32, kind="ExternalInput").ap()
    wqL = dt("wqL", [DC, 128, D], F32, kind="ExternalInput").ap()
    wkL = dt("wkL", [DC, 128, D], F32, kind="ExternalInput").ap()
    wvL = dt("wvL", [DC, 128, D], F32, kind="ExternalInput").ap()
    woL = dt("woL", [DC, 128, D], F32, kind="ExternalInput").ap()
    biases = dt("biases", [4, D], F32, kind="ExternalInput").ap()
    wguL = dt("wguL", [NH, 128, 2 * D], I32, kind="ExternalInput").ap()
    wdL = dt("wdL", [NH, 128, D], I32, kind="ExternalInput").ap()
    caq = dt("caq", [128, 2 * T], F32, kind="ExternalInput").ap()
    cak = dt("cak", [128, 2 * T], F32, kind="ExternalInput").ap()
    ident = dt("ident", [128, 128], F32, kind="ExternalInput").ap()
    permAB = dt("permAB", [128, 256], BF16, kind="ExternalInput").ap()
    fconsts = dt("fconsts", [1, 16], F32, kind="ExternalInput").ap()

    y_out = dt("y_out", [T, D], F32, kind="ExternalOutput").ap()
    dbg_out = dt("dbg_out", [128, 32768], F32, kind="ExternalOutput").ap() if dbg else None
    dbg_out2 = dt("dbg_out2", [128, 32768], F32, kind="ExternalOutput").ap() if dbg else None

    x1sp = dt("x1sp", [128, DC * T], F32)
    x2sp = dt("x2sp", [128, DC * T], F32)
    v_snd = dt("v_snd", [128, 16384], BF16)
    v_rcv = dt("v_rcv", [256, 16384], BF16)
    k_snd = dt("k_snd", [128, 16384], BF16)
    k_rcv = dt("k_rcv", [256, 16384], BF16)
    rbd = dt("rbd", [2, QT], F32)

    arena = nc.alloc_sbuf_tensor("arena", [128, 52900], F32)
    A0 = nc.lookup_mls("arena").memorylocations[0].addr
    A0 = (A0 + 31) // 32 * 32

    def at(name, shape, dtyp, off):
        return nc.alloc_sbuf_tensor_at(name, shape, dtyp, offset=A0 + off)

    R0, R1, R2, R3 = 0, 64 * KB, 128 * KB, 192 * KB
    xs = at("xs", [128, DC * T], F32, R0)
    xln2 = at("xln2", [128, DC * T], BF16, R0)
    x2stg = at("x2stg", [128, DC * QT], F32, R0)
    nsq = at("nsq", [128, S], BF16, R0 + 16 * KB)
    attn_fm = at("attn_fm", [128, DC * QT], BF16, R0 + 16 * KB)
    pT = at("pT", [128, 2 * 2 * QT], BF16, R0 + 24 * KB)
    woS = at("woS", [128, 2 * D], BF16, R0 + 28 * KB)
    qrope = at("qrope", [128, DC * T], BF16, R0 + 32 * KB)
    kfull = at("kfull", [128, DC * S], BF16, R1)
    hbuf = at("hbuf", [128, NH * TG], BF16, R1)
    xlnf = at("xlnf", [128, DC * TG], BF16, R1 + 48 * KB)
    kstg = at("kstg", [128, DC * T], BF16, R1)
    wfull = at("wfull", [128, DC * D], BF16, R1 + 32 * KB)
    qraw = at("qraw", [128, T], BF16, R1 + 48 * KB)
    rtmp = at("rtmp", [128, T], F32, R1 + 52 * KB)
    vfull = at("vfull", [128, KC * KB], BF16, R2)
    vstg = at("vstg", [128, 16 * KB], BF16, R2)
    m2row = at("m2row", [1, T], F32, R2)
    rows = at("rows", [1, 2 * T], F32, R2 + 8 * KB)
    xbf = at("xbf", [128, 2 * T], BF16, R2 + 24 * KB)
    sqbf = at("sqbf", [128, 2 * T], BF16, R2 + 32 * KB)
    wgu_s = at("wgu_s", [128, 2 * 4 * D], BF16, R2 + 24 * KB)
    wd_s = at("wd_s", [128, 2 * 4 * D], BF16, R2 + 40 * KB)
    lntmp = at("lntmp", [128, 512], F32, R2 + 56 * KB)
    silu_g = at("silu_g", [128, 2 * TG], BF16, R2 + 58 * KB)
    stg = at("stg", [128, D], F32, R1 + 48 * KB)
    stg2 = at("stg2", [128, D], F32, R1 + 52 * KB)
    cab = at("cab", [128, 2 * T], F32, R2 + 32 * KB)
    cs_idf = at("cs_idf", [128, 128], F32, R3)
    cs_perm = at("cs_perm", [128, 256], BF16, R3 + 512)
    onesc_bf = at("onesc_bf", [128, 1], BF16, R3 + 1024)
    onesr_bf = at("onesr_bf", [1, T], BF16, R3 + 1056)
    onesf = at("onesf", [65, 128], F32, R3 + 1056 + 4 * KB)
    negs = at("negs", [1, 128], F32, R3 + 1568 + 4 * KB)
    biasb = at("biasb", [1, D], BF16, R3 + 2080 + 4 * KB)
    nrm = at("nrm", [1, 64], F32, R3 + 2080 + 6 * KB)
    negC = at("negC", [128, NPAIR], F32, R3 + 2336 + 6 * KB)
    dens = at("dens", [65, QT], F32, R3 + 2368 + 6 * KB)
    recips = at("recips", [65, QT], F32, R3 + 2368 + 8 * KB)
    rbs = at("rbs", [128, QT], F32, R3 + 2368 + 10 * KB)
    fc = at("fc", [1, 16], F32, R3 + 2368 + 12 * KB)

    for i_, v_ in enumerate((EPS, WS)):
        nc.const_aps.register(F32, v_, fc[0:1, i_:i_ + 1])

    parena = nc.alloc_psum_tensor("parena", [128, 4096], F32)

    def bank(b, n=512):
        return parena[:, b * 512:b * 512 + n]

    sems, cnt = {}, {}
    for s_ in ("ld", "lw", "sp", "cc", "pe", "ac", "dv", "gp", "ldc", "ldx",
               "lwg", "lwd", "lww", "lwb", "lwo", "cabq", "snd", "kin", "vin",
               "x2l", "x2m", "bcd", "x2s", "sty", "sty2", "ldy", "sp1", "dbg"):
        sems[s_] = nc.alloc_semaphore(s_)
        cnt[s_] = 0

    def inc(instr, name, by=1):
        instr.then_inc(sems[name], by)
        cnt[name] += by
        return cnt[name]

    def val(name):
        return cnt[name]

    def wait(eng, name, v):
        if v > 0:
            eng.wait_ge(sems[name], v)

    def chain(e, m):
        wait(e, "dv", inc(m, "dv"))

    def dgo(e, name, out, in_):
        wait(e, name, cnt[name])
        m = e.dma_start(out=out, in_=in_)
        m.then_inc(sems[name], 16)
        cnt[name] += 16
        return m

    def mark(key, name):
        if dry:
            M[key] = (name, cnt[name])
        else:
            assert M[key] == (name, cnt[name]), (key, M[key], (name, cnt[name]))

    def wt(eng, key):
        if key not in M:
            assert dry or dbg, "unknown mark %r in final pass" % (key,)
            return
        nm, v = M[key]
        wait(eng, nm, v)

    def xs_c(c, t0, t1):
        return xs[:, c * T + t0:c * T + t1]

    NQC_E = 1 if dbg in ("attn1", "attn1b", "attn2") else NQC
    NPAIR_E = 1 if dbg == "attn1" else NPAIR
    SKIP_OP = dbg in ("attn1", "attn1b")

    with nc.Block() as blk:

        def pro_sync(e):
            dgo(e, "ldc", cs_idf[:], ident[:, :])
            dgo(e, "ldc", cs_perm[:], permAB[:, :])
            dgo(e, "ldc", fc[:], fconsts[:, :])
            mark("consts", "ldc")
        blk.sync(pro_sync)

        def pro_dv(e):
            e.memset(onesc_bf[:], 1.0)
            e.memset(onesr_bf[:], 1.0)
            e.memset(onesf[:], 1.0)
            e.memset(negs[:], -0.5)
            e.memset(nrm[:], 0.0)
            inc(e.memset(parena[:, 5 * 512:6 * 512], 1.0), "dv")
            mark("minit", "dv")
        blk.vector(pro_dv)

        def pro_ac(e):
            wt(e, "consts")
        blk.scalar(pro_ac)

        def pro_dv2(e):
            wt(e, "consts")
        blk.vector(pro_dv2)

        def pro_pe(e):
            wt(e, "consts")
            wt(e, "minit")
            v = inc(e.matmul(parena[0:32, 0:32], cs_idf[:, 0:32], cs_idf[:, 0:32],
                             start=True, stop=True), "pe")
            wait(e, "pe", v)
        blk.tensor(pro_pe)

        def load_sync(e, par, fam):
            for tt in range(par, NTT, 2):
                sg = stg if tt % 2 == 0 else stg2
                if tt >= 2:
                    wt(e, ("ldT", tt - 2))
                dgo(e, fam, sg[:], x_in[tt * 128:(tt + 1) * 128, :])
                mark(("ldL", tt), fam)
        blk.sync(lambda e: load_sync(e, 0, "ldx"))
        blk.scalar(lambda e: load_sync(e, 1, "ldy"))

        def load_pe(e):
            for tt in range(NTT):
                sg = stg if tt % 2 == 0 else stg2
                wt(e, ("ldL", tt))
                m = None
                for c in range(DC):
                    if c == 0 and tt >= 1:
                        wt(e, ("ldE4", tt - 1))
                    if c == 4 and tt >= 1:
                        wt(e, ("ldE", tt - 1))
                    m = e.transpose(parena[:, c * 512:c * 512 + 128],
                                    sg[:, c * 128:(c + 1) * 128], cs_idf[:])
                    if c == 3:
                        inc(m, "pe")
                        mark(("ldT4", tt), "pe")
                inc(m, "pe")
                mark(("ldT", tt), "pe")
            mark("ldone_pe", "pe")
        blk.tensor(load_pe)

        def load_dv(e):
            for tt in range(NTT):
                wt(e, ("ldT4", tt))
                m = None
                for c in range(DC):
                    if c == 4:
                        wt(e, ("ldT", tt))
                    m = e.tensor_copy(xs_c(c, tt * 128, (tt + 1) * 128),
                                      parena[:, c * 512:c * 512 + 128])
                    if c == 3:
                        inc(m, "dv")
                        mark(("ldE4", tt), "dv")
                inc(m, "dv")
                mark(("ldE", tt), "dv")
            mark("ldone", "dv")
        blk.vector(load_dv)

        def emit_ln_stats(tag, ready_keys, chunk_keys=None):
            def dv_b(e):
                for k in ready_keys:
                    wt(e, k)
                for c in range(DC):
                    h = c % 2
                    if chunk_keys:
                        for k in chunk_keys(c):
                            wt(e, k)
                    if c >= 2:
                        wt(e, (tag, "mm", c - 2))
                    m = e.tensor_copy(xbf[:, h * T:(h + 1) * T], xs_c(c, 0, T))
                    inc(m, "dv")
                    mark((tag, "cast", c), "dv")
            blk.vector(dv_b)

            def ac_b(e):
                for c in range(DC):
                    h = c % 2
                    wt(e, (tag, "cast", c))
                    if c >= 2:
                        wt(e, (tag, "mm", c - 2))
                    m = e.activation(out=sqbf[:, h * T:(h + 1) * T],
                                     in_=xbf[:, h * T:(h + 1) * T], func=AF.Square)
                    inc(m, "ac")
                    mark((tag, "sq", c), "ac")
            blk.scalar(ac_b)

            def pe_b(e):
                for c in range(DC):
                    h = c % 2
                    wt(e, (tag, "sq", c))
                    m = None
                    for ts in range(4):
                        e.matmul(parena[0:1, ts * 512:(ts + 1) * 512], onesc_bf[:],
                                 xbf[:, h * T + ts * 512:h * T + (ts + 1) * 512],
                                 start=(c == 0), stop=(c == DC - 1), tile_position=(0, 0))
                        m = e.matmul(parena[0:1, (4 + ts) * 512:(5 + ts) * 512], onesc_bf[:],
                                     sqbf[:, h * T + ts * 512:h * T + (ts + 1) * 512],
                                     start=(c == 0), stop=(c == DC - 1), tile_position=(0, 0))
                    inc(m, "pe")
                    mark((tag, "mm", c), "pe")
            blk.tensor(pe_b)

            def ac2_b(e):
                wt(e, (tag, "mm", DC - 1))
                m = e.copy(out=rows[0:1, :], in_=parena[0:1, 0:4096])
                inc(m, "ac")
                mark((tag, "statc"), "ac")
            blk.scalar(ac2_b)

            def dv2_b(e):
                wt(e, (tag, "statc"))
                chain(e, e.tensor_scalar(out=rows[0:1, :], in0=rows[0:1, :],
                                         scalar1=1.0 / D, scalar2=None, op0=ALU.mult))
                chain(e, e.tensor_tensor(out=m2row[0:1, :], in0=rows[0:1, 0:T],
                                         in1=rows[0:1, 0:T], op=ALU.mult))
                m = e.tensor_tensor(out=rows[0:1, T:2 * T], in0=rows[0:1, T:2 * T],
                                    in1=m2row[0:1, :], op=ALU.subtract)
                inc(m, "dv")
                mark((tag, "var"), "dv")
            blk.vector(dv2_b)

            def ac3_b(e):
                wt(e, (tag, "var"))
                m = e.activation(out=rows[0:1, T:2 * T], in_=rows[0:1, T:2 * T],
                                 func=AF.Sqrt, bias=EPS)
                inc(m, "ac")
                mark((tag, "sdev"), "ac")
            blk.scalar(ac3_b)

            def dv3_b(e):
                wt(e, (tag, "sdev"))
                m = e.reciprocal(rows[0:1, T:2 * T], rows[0:1, T:2 * T])
                inc(m, "dv")
                mark((tag, "rstd"), "dv")
            blk.vector(dv3_b)

        def emit_ln_mat(tag, ln, dst, dstw, t0, t1, pre_keys=()):
            nts = (t1 - t0) // 512

            def pe_b(e):
                wt(e, (ln, "rstd"))
                for k in pre_keys:
                    wt(e, k)
                m = None
                for ts in range(nts):
                    e.matmul(bank(ts), onesf[0:1, :],
                             rows[0:1, t0 + ts * 512:t0 + (ts + 1) * 512],
                             start=True, stop=True, tile_position=(0, 0))
                    m = e.matmul(bank(nts + ts), onesf[0:1, :],
                                 rows[0:1, T + t0 + ts * 512:T + t0 + (ts + 1) * 512],
                                 start=True, stop=True, tile_position=(0, 0))
                inc(m, "pe")
                mark((tag, "bc"), "pe")
            blk.tensor(pe_b)

            def dv_b(e):
                wt(e, (tag, "bc"))
                for c in range(DC):
                    m = None
                    for ts in range(nts):
                        chain(e, e.tensor_tensor(
                            out=lntmp[:, :],
                            in0=xs_c(c, t0 + ts * 512, t0 + (ts + 1) * 512),
                            in1=bank(ts), op=ALU.subtract))
                        m = e.tensor_tensor(
                            out=dst[:, c * dstw + ts * 512:c * dstw + (ts + 1) * 512],
                            in0=lntmp[:, :], in1=bank(nts + ts), op=ALU.mult)
                        chain(e, m)
                    mark((tag, "mat", c), "dv")
                mark((tag, "matdone"), "dv")
            blk.vector(dv_b)

        def emit_ffn(ph, gate_keys=()):
            ln = "ln%d" % ph
            for g in range(NTG):
                t0 = g * TG
                ftag = "f%d%d" % (ph, g)
                mtag = ftag + "m"
                prevf = "f%d%d" % (ph, g - 1) if g > 0 else \
                        ("f1%d" % (NTG - 1) if ph == 3 else None)
                pre = []
                if g > 0:
                    pre = [("f%d%d" % (ph, g - 1), "res", 1, DC - 1)]
                emit_ln_mat(mtag, ln, xlnf, TG, t0, t0 + TG, pre_keys=pre)

                def gp_b(e, ftag=ftag, prevf=prevf, g=g, ph=ph):
                    for k_ in gate_keys if g == 0 else ():
                        wt(e, k_)
                    if g == 0:
                        wt(e, (ln, "statc"))
                    for p in range(NH // 2):
                        if p >= 2:
                            wt(e, (ftag, "gu", 2 * p - 3))
                        elif prevf is not None:
                            wt(e, (prevf, "gu", NH - 4 + 2 * p + 1))
                        dgo(e, "lwg", wgu_s[:, (p % 2) * 4 * D:(p % 2 + 1) * 4 * D],
                            wguL[2 * p:2 * p + 2, :, :].rearrange("a p e -> p a e"))
                        mark((ftag, "wgu", p), "lwg")
                    for ts in range(2):
                        for p in range(NH // 4):
                            if p >= 2:
                                wt(e, (ftag, "dn", ts, 4 * p - 5))
                            elif ts == 1:
                                wt(e, (ftag, "dn", 0, NH - 8 + 4 * p + 3))
                            elif prevf is not None:
                                wt(e, (prevf, "dn", 1, NH - 8 + 4 * p + 3))
                            dgo(e, "lwd", wd_s[:, (p % 2) * 4 * D:(p % 2 + 1) * 4 * D],
                                wdL[4 * p:4 * p + 4, :, :].rearrange("a p e -> p a e"))
                            mark((ftag, "wd", ts, p), "lwd")
                blk.gpsimd(gp_b)

                def pe_b(e, ftag=ftag, mtag=mtag):
                    for k in range(NH):
                        wt(e, (ftag, "wgu", k // 2))
                        if k == 0:
                            wt(e, (mtag, "matdone"))
                        if k >= 2:
                            wt(e, (ftag, "silu", k - 2))
                            wt(e, (ftag, "hmul", k - 2))
                        for c in range(DC):
                            for ts in range(2):
                                m = e.matmul(
                                    bank(4 * (k % 2) + ts),
                                    wgu_s[:, ((k // 2) % 2) * 4 * D + (k % 2) * 2 * D + c * 128:
                                          ((k // 2) % 2) * 4 * D + (k % 2) * 2 * D + (c + 1) * 128],
                                    xlnf[:, c * TG + ts * 512:c * TG + (ts + 1) * 512],
                                    start=(c == 0), stop=(c == DC - 1))
                        inc(m, "pe")
                        mark((ftag, "gmm", k), "pe")
                        for c in range(DC):
                            for ts in range(2):
                                m = e.matmul(
                                    bank(4 * (k % 2) + 2 + ts),
                                    wgu_s[:, ((k // 2) % 2) * 4 * D + (k % 2) * 2 * D + D + c * 128:
                                          ((k // 2) % 2) * 4 * D + (k % 2) * 2 * D + D + (c + 1) * 128],
                                    xlnf[:, c * TG + ts * 512:c * TG + (ts + 1) * 512],
                                    start=(c == 0), stop=(c == DC - 1))
                        inc(m, "pe")
                        mark((ftag, "gu", k), "pe")
                    for ts in range(2):
                        for k in range(NH):
                            wt(e, (ftag, "wd", ts, k // 4))
                            if ts == 0 and k == 0:
                                wt(e, (ftag, "hmul", NH - 1))
                                wt(e, (ftag, "hmul", NH - 2))
                            if ts == 1 and k == 0:
                                for dd in range(DC):
                                    wt(e, (ftag, "res", 0, dd))
                            for dout in range(DC):
                                m = e.matmul(
                                    bank(dout),
                                    wd_s[:, ((k // 4) % 2) * 4 * D + (k % 4) * D + dout * 128:
                                          ((k // 4) % 2) * 4 * D + (k % 4) * D + (dout + 1) * 128],
                                    hbuf[:, k * TG + ts * 512:k * TG + (ts + 1) * 512],
                                    start=(k == 0), stop=(k == NH - 1))
                            inc(m, "pe")
                            mark((ftag, "dn", ts, k), "pe")
                blk.tensor(pe_b)

                def ac_b(e, ftag=ftag):
                    for k in range(NH):
                        wt(e, (ftag, "gmm", k))
                        if k >= 2:
                            wt(e, (ftag, "hmul", k - 2))
                        m = e.activation(out=silu_g[:, (k % 2) * TG:(k % 2 + 1) * TG],
                                         in_=bank(4 * (k % 2), n=1024), func=AF.Silu,
                                         scale=WS)
                        inc(m, "ac")
                        mark((ftag, "silu", k), "ac")
                blk.scalar(ac_b)

                def dv_b(e, ftag=ftag, t0=t0):
                    for k in range(NH):
                        wt(e, (ftag, "silu", k))
                        wt(e, (ftag, "gu", k))
                        m = e.tensor_tensor(out=hbuf[:, k * TG:(k + 1) * TG],
                                            in0=silu_g[:, (k % 2) * TG:(k % 2 + 1) * TG],
                                            in1=bank(4 * (k % 2) + 2, n=1024), op=ALU.mult)
                        inc(m, "dv")
                        mark((ftag, "hmul", k), "dv")
                    for ts in range(2):
                        wt(e, (ftag, "dn", ts, NH - 1))
                        for dout in range(DC):
                            m = e.scalar_tensor_tensor(
                                out=xs_c(dout, t0 + ts * 512, t0 + (ts + 1) * 512),
                                in0=bank(dout), scalar=0.5 * WS * WS,
                                in1=xs_c(dout, t0 + ts * 512, t0 + (ts + 1) * 512),
                                op0=ALU.mult, op1=ALU.add)
                            inc(m, "dv")
                            mark((ftag, "res", ts, dout), "dv")
                blk.vector(dv_b)
            mark(("ffn%d" % ph, "dv"), "dv")
            mark(("ffn%d" % ph, "pe"), "pe")

        emit_ln_stats("ln1", ["ldone"])
        emit_ffn(1)

        if dbg == "ffn1":
            def dump1(e):
                wt(e, ("ffn1", "dv"))
                dgo(e, "dbg", dbg_out[:, 0:DC * T], xs[:])
                dgo(e, "dbg", dbg_out2[0:1, 0:2 * T], rows[:, :])
                dgo(e, "dbg", dbg_out2[0:1, 2 * T:3 * T], m2row[:, :])
                wait(e, "dbg", val("dbg"))
            blk.sync(dump1)
            return nc

        emit_ln_stats("ln2", [("ffn1", "dv")])

        def spill_x1(e):
            wt(e, ("ln2", "cast", DC - 1))
            wt(e, ("ffn1", "dv"))
            dgo(e, "sp1", x1sp[:, :], xs[:])
            mark("x1spilled", "sp1")
        blk.sync(spill_x1)

        def spgate(e):
            wt(e, "x1spilled")
            wait(e, "dv", val("dv"))
            inc(e.nop(), "dv")
            mark("spgate", "dv")
        blk.vector(spgate)

        emit_ln_mat("ln2m", "ln2", xln2, T, 0, T, pre_keys=["spgate"])

        if dbg == "xln2":
            def dumpx(e):
                wt(e, ("ln2m", "matdone"))
                dgo(e, "dbg", dbg_out[:, 0:DC * T], xln2[:])
                e.wait_ge(sems["dbg"], cnt["dbg"])
            blk.gpsimd(dumpx)
            return

        def pw_gp(e, rng):
            if 0 in rng:
                wt(e, ("ffn1", "pe"))
            for i in rng:
                wl = (wkL, wvL, wqL)[i]
                if i > 0:
                    wt(e, ("pj", "wfree", i - 1))
                dgo(e, "lww", wfull[:, :], wl[:, :, :].rearrange("c p e -> p c e"))
                mark(("pj", "w", i), "lww")
                if i > 0:
                    wt(e, ("pj", "bfree", i - 1))
                dgo(e, "lwb", biasb[:, :], biases[i:i + 1, :])
                mark(("pj", "b", i), "lwb")
        blk.gpsimd(lambda e: pw_gp(e, (0, 1)))

        def pwo_gp(e):
            wt(e, ("pj", "bfree", 2))
            dgo(e, "lwb", biasb[:, :], biases[3:4, :])
            mark(("pj", "b", 3), "lwb")

        def wo_gp(e):
            wt(e, ("qp", "ropedone"))

            def wo_load(qt, dout):
                idx = qt * DC + dout
                if idx >= 2:
                    wt(e, ("opmm", (idx - 2) // DC, (idx - 2) % DC))
                dgo(e, "lwo", woS[:, (idx % 2) * D:(idx % 2 + 1) * D],
                    woL[:, :, dout * 128:(dout + 1) * 128].rearrange("c p j -> p c j"))
                mark(("wo", qt, dout), "lwo")

            if not SKIP_OP:
                wo_load(0, 0)
                wo_load(0, 1)
            for qt in range(NQC_E):
                if not SKIP_OP:
                    for dout in range(2, DC):
                        wo_load(qt, dout)
                    if qt + 1 < NQC_E:
                        wo_load(qt + 1, 0)
                        wo_load(qt + 1, 1)

        def pw_sync(e):
            wt(e, ("ln2", "statc"))
            dgo(e, "cabq", cab[:, :], cak[:, :])
            mark("cab_k", "cabq")
        blk.sync(pw_sync)

        def emit_proj_fm(idx, tagp, dst, prev_done_key, cab_key):
            def pe_b(e):
                wt(e, ("pj", "w", idx))
                wt(e, ("pj", "b", idx))
                wt(e, prev_done_key)
                for pc in range(DC):
                    if pc >= 1:
                        wt(e, (tagp, "qev", pc - 1))
                    for c in range(DC):
                        for ts in range(4):
                            e.matmul(bank(ts),
                                     wfull[:, c * D + pc * 128:c * D + (pc + 1) * 128],
                                     xln2[:, c * T + ts * 512:c * T + (ts + 1) * 512],
                                     start=(c == 0), stop=False)
                    m = None
                    for ts in range(4):
                        m = e.matmul(bank(ts), biasb[:, pc * 128:(pc + 1) * 128],
                                     onesr_bf[0:1, ts * 512:(ts + 1) * 512],
                                     start=False, stop=True)
                    inc(m, "pe")
                    mark((tagp, "qmm", pc), "pe")
                    wt(e, (tagp, "qev", pc))
                    for th in range(2):
                        if pc >= 1 or th >= 1:
                            prev = (tagp, "rope", pc, th - 1) if th >= 1 else (tagp, "rope", pc - 1, 1)
                            wt(e, prev)
                        for hh in range(2):
                            e.matmul(bank(4 + hh), cs_perm[:, 0:128],
                                     qraw[:, th * KB + hh * 512:th * KB + (hh + 1) * 512],
                                     start=True, stop=True)
                            m = e.matmul(bank(6 + hh), cs_perm[:, 128:256],
                                         qraw[:, th * KB + hh * 512:th * KB + (hh + 1) * 512],
                                         start=True, stop=True)
                        inc(m, "pe")
                        mark((tagp, "perm", pc, th), "pe")
                mark(("pj", "wfree", idx), "pe")
                mark(("pj", "bfree", idx), "pe")
            blk.tensor(pe_b)

            def ac_b(e):
                for pc in range(DC):
                    wt(e, (tagp, "qmm", pc))
                    if pc >= 1:
                        wt(e, (tagp, "perm", pc - 1, 1))
                    m = e.activation(out=qraw[:, :], in_=parena[:, 0:2048], func=AF.Copy)
                    inc(m, "ac")
                    mark((tagp, "qev", pc), "ac")
            blk.scalar(ac_b)

            def dv_b(e):
                wt(e, cab_key)
                for pc in range(DC):
                    for th in range(2):
                        wt(e, (tagp, "perm", pc, th))
                        m = None
                        for hh in range(2):
                            sl = th * KB + hh * 512
                            e.tensor_tensor(out=rtmp[:, 0:512], in0=bank(4 + hh),
                                            in1=cab[:, sl:sl + 512], op=ALU.mult)
                            chain(e, e.tensor_tensor(out=rtmp[:, 512:1024], in0=bank(6 + hh),
                                                     in1=cab[:, T + sl:T + sl + 512],
                                                     op=ALU.mult))
                            m = e.tensor_tensor(out=dst[:, pc * T + sl:pc * T + sl + 512],
                                                in0=rtmp[:, 0:512], in1=rtmp[:, 512:1024],
                                                op=ALU.add)
                            chain(e, m)
                        mark((tagp, "rope", pc, th), "dv")
                mark((tagp, "ropedone"), "dv")
            blk.vector(dv_b)

        emit_proj_fm(0, "kp", kstg, ("ln2m", "matdone"), "cab_k")

        def cabq_sync(e):
            wt(e, ("kp", "ropedone"))
            dgo(e, "cabq", cab[:, :], caq[:, :])
            mark("cab_q", "cabq")
        blk.sync(cabq_sync)

        def vproj_pe(e):
            wt(e, ("pj", "w", 1))
            wt(e, ("pj", "b", 1))
            wt(e, ("kp", "ropedone"))
            for tt in range(NTT):
                bs = 4 + 2 * (tt % 2)
                if tt >= 2:
                    wt(e, ("pj", "vev", tt - 2))
                for c in range(DC):
                    for eh in range(2):
                        e.matmul(bank(bs + eh),
                                 xln2[:, c * T + tt * 128:c * T + (tt + 1) * 128],
                                 wfull[:, c * D + eh * 512:c * D + (eh + 1) * 512],
                                 start=(c == 0), stop=False)
                for eh in range(2):
                    m = e.matmul(bank(bs + eh), onesr_bf[0:1, 0:128],
                                 biasb[:, eh * 512:(eh + 1) * 512], start=False, stop=True)
                inc(m, "pe")
                mark(("pj", "vmm", tt), "pe")
            mark(("pj", "wfree", 1), "pe")
            mark(("pj", "bfree", 1), "pe")
        blk.tensor(vproj_pe)

        def vproj_ac(e):
            for tt in range(NTT):
                bs = 4 + 2 * (tt % 2)
                wt(e, ("pj", "vmm", tt))
                m = e.activation(out=vstg[:, tt * KB:(tt + 1) * KB],
                                 in_=parena[:, bs * 512:(bs + 2) * 512], func=AF.Copy)
                inc(m, "ac")
                mark(("pj", "vev", tt), "ac")
            mark("vdone", "ac")
        blk.scalar(vproj_ac)

        emit_proj_fm(2, "qp", qrope, "vdone", "cab_q")

        if dbg == "qkv":
            def dumpq(e):
                wt(e, ("qp", "ropedone"))
                wt(e, "vdone")
                dgo(e, "dbg", dbg_out[:, 0:DC * T], qrope[:])
                dgo(e, "dbg", dbg_out[:, DC * T:2 * DC * T], kstg[:])
                dgo(e, "dbg", dbg_out2[:, 0:16 * KB], vstg[:])
                e.wait_ge(sems["dbg"], cnt["dbg"])
            blk.gpsimd(dumpq)
            return

        def ag_sync(e):
            wt(e, ("kp", "ropedone"))
            dgo(e, "snd", k_snd[:, :], kstg[:, :])
            mark("kout", "snd")
            wt(e, "vdone")
            dgo(e, "snd", v_snd[:, :], vstg[:, :])
            mark("vout", "snd")
        blk.sync(ag_sync)

        def agK_gp(e):
            wt(e, "kout")
            m = e.collective_compute(
                "AllGather", ALU.bypass,
                replica_groups=[[0, 1], [2, 3], [4, 5], [6, 7]],
                ins=[k_snd.ap().opt()], outs=[k_rcv.ap().opt()])
            inc(m, "cc")
            mark("agK", "cc")
        blk.gpsimd(agK_gp)

        blk.gpsimd(lambda e: pw_gp(e, (2,)))

        def agV_gp(e):
            wt(e, "vout")
            m = e.collective_compute(
                "AllGather", ALU.bypass,
                replica_groups=[[0, 1], [2, 3], [4, 5], [6, 7]],
                ins=[v_snd.ap().opt()], outs=[v_rcv.ap().opt()])
            inc(m, "cc")
            mark("agV", "cc")
        blk.gpsimd(agV_gp)

        blk.gpsimd(pwo_gp)

        def ag_in_sync(e):
            wt(e, "agK")
            for c in range(4):
                for r in range(2):
                    dgo(e, "kin", kfull[:, c * S + r * T:c * S + r * T + T],
                        k_rcv[r * 128:(r + 1) * 128, c * T:(c + 1) * T])
                mark(("kin", c), "kin")
            wt(e, "agV")
            for tc in range(16):
                dgo(e, "vin", vfull[:, tc * KB:(tc + 1) * KB],
                    v_rcv[0:128, tc * KB:(tc + 1) * KB])
            wt(e, ("qp", "qmm", DC - 1))
            for c in range(4, 6):
                for r in range(2):
                    dgo(e, "kin", kfull[:, c * S + r * T:c * S + r * T + T],
                        k_rcv[r * 128:(r + 1) * 128, c * T:(c + 1) * T])
                mark(("kin", c), "kin")
            wt(e, ("qp", "ropedone"))
            for c in range(6, 8):
                for r in range(2):
                    dgo(e, "kin", kfull[:, c * S + r * T:c * S + r * T + T],
                        k_rcv[r * 128:(r + 1) * 128, c * T:(c + 1) * T])
                mark(("kin", c), "kin")
            mark("kvin", "kin")
            for tc in range(16):
                dgo(e, "vin", vfull[:, (16 + tc) * KB:(17 + tc) * KB],
                    v_rcv[128:256, tc * KB:(tc + 1) * KB])
            mark("vin", "vin")
        blk.sync(ag_in_sync)

        def nrm_ac(e):
            wt(e, ("qp", "ropedone"))
            for pc in range(DC):
                h = pc % 2
                if pc >= 2:
                    wt(e, ("nqm", pc - 2))
                m = e.activation(out=nsq[:, h * T:(h + 1) * T],
                                 in_=qrope[:, pc * T:(pc + 1) * T], func=AF.Square)
                inc(m, "ac")
                mark(("nq", pc), "ac")
            for pc in range(DC):
                wt(e, ("kin", pc))
                if pc == 0:
                    wt(e, ("nqm", DC - 1))
                else:
                    wt(e, ("nkm", pc - 1))
                m = e.activation(out=nsq[:, 0:S], in_=kfull[:, pc * S:(pc + 1) * S],
                                 func=AF.Square)
                inc(m, "ac")
                mark(("nk", pc), "ac")
        blk.scalar(nrm_ac)

        def nrm_pe(e):
            for pc in range(DC):
                h = pc % 2
                R = 32 * (pc // 2)
                wt(e, ("nq", pc))
                m = None
                for ts in range(4):
                    m = e.matmul(parena[R:R + 1, (h * 4 + ts) * 512:(h * 4 + ts + 1) * 512],
                                 onesc_bf[:], nsq[:, (h * 4 + ts) * 512:(h * 4 + ts + 1) * 512],
                                 start=True, stop=True, tile_position=(0, R))
                inc(m, "pe")
                mark(("nqm", pc), "pe")
            for pc in range(DC):
                R = 32 * (pc % 4)
                wt(e, ("nk", pc))
                if pc < 4:
                    wt(e, ("nqr", 2 * pc))
                    wt(e, ("nqr", 2 * pc + 1))
                else:
                    wt(e, ("nkr", pc - 4))
                m = None
                for ts in range(8):
                    m = e.matmul(parena[R:R + 1, ts * 512:(ts + 1) * 512],
                                 onesc_bf[:], nsq[:, ts * 512:(ts + 1) * 512],
                                 start=True, stop=True, tile_position=(0, R))
                inc(m, "pe")
                mark(("nkm", pc), "pe")
            wt(e, ("nsum",))
            m = e.matmul(parena[:, 0:8], negs[:, :], nrm[0:1, 32:40],
                         start=True, stop=True)
            inc(m, "pe")
            mark("negCm", "pe")
        blk.tensor(nrm_pe)

        def nrm_red(e, par, sem):
            ax = mybir.AxisListType.X
            for pc in range(par, DC, 1 if par == 0 else 9):
                h = pc % 2
                R = 32 * (pc // 2)
                wt(e, ("nqm", pc))
                m = e.tensor_reduce(out=nrm[0:1, pc:pc + 1],
                                    in_=parena[R:R + 1, h * 2048:(h + 1) * 2048],
                                    axis=ax, op=ALU.max)
                wait(e, sem, inc(m, sem))
                mark(("nqr", pc), sem)
            for pc in range(par, DC, 1 if par == 0 else 9):
                R = 32 * (pc % 4)
                wt(e, ("nkm", pc))
                m = e.tensor_reduce(out=nrm[0:1, 16 + pc:17 + pc],
                                    in_=parena[R:R + 1, 0:4096],
                                    axis=ax, op=ALU.max)
                wait(e, sem, inc(m, sem))
                mark(("nkr", pc), sem)
        blk.vector(lambda e: nrm_red(e, 0, "dv"))

        def nrm_gp(e):
            wt(e, ("nkr", DC - 1))
            m = e.tensor_tensor(out=nrm[0:1, 32:48], in0=nrm[0:1, 0:16],
                                in1=nrm[0:1, 16:32], op=ALU.add)
            inc(m, "gp")
            mark(("nsum",), "gp")
        blk.gpsimd(nrm_gp)

        def nrm_ac2(e):
            wt(e, "negCm")
            m = e.copy(out=negC[:, :], in_=parena[:, 0:8])
            inc(m, "ac")
            mark("negCr", "ac")
        blk.scalar(nrm_ac2)

        blk.gpsimd(wo_gp)

        if dbg == "kv":
            def dumpkv(e):
                wt(e, "kvin")
                wt(e, "vin")
                wt(e, "negCr")
                dgo(e, "dbg", dbg_out[:, :], kfull[:])
                dgo(e, "dbg", dbg_out2[:, 0:KC * KB], vfull[:])
                dgo(e, "dbg", dbg_out2[:, KC * KB - NPAIR:KC * KB], negC[:])
                dgo(e, "dbg", dbg_out2[0:1, KC * KB - 72:KC * KB - 8], nrm[:])
                e.wait_ge(sems["dbg"], cnt["dbg"])
            blk.gpsimd(dumpkv)
            return

        OPB = 7

        def attn_pe(e):
            wt(e, "kvin")
            wt(e, "vin")
            wt(e, "negCr")

            def emit_outproj(e, qt, dout):
                wt(e, ("pj", "b", 3))
                wt(e, ("epiTT", qt * NPAIR + NPAIR - 1))
                idx = qt * DC + dout
                wt(e, ("wo", qt, dout))
                if dout >= 1:
                    wt(e, ("opev", qt, dout - 1))
                for c in range(DC):
                    e.matmul(bank(OPB),
                             woS[:, (idx % 2) * D + c * 128:(idx % 2) * D + (c + 1) * 128],
                             attn_fm[:, c * QT:(c + 1) * QT],
                             start=(c == 0), stop=False)
                m = e.matmul(bank(OPB), biasb[:, dout * 128:(dout + 1) * 128],
                             onesr_bf[0:1, 0:QT], start=False, stop=True)
                inc(m, "pe")
                mark(("opmm", qt, dout), "pe")

            for qt in range(NQC_E):
                for pr in range(NPAIR_E):
                    it0 = (qt * NPAIR + pr)
                    pvb = 4 + 2 * (pr % 2)
                    dnb = pvb + 1
                    for kc in range(KC + 1):
                        if kc < KC:
                            sb = 2 * (kc % 2)
                            if kc == 0:
                                if it0 >= 2:
                                    wt(e, ("epiTT", it0 - 2))
                                if pr == 1 and qt > 0 and not SKIP_OP:
                                    wt(e, ("opev", qt - 1, DC - 1))
                            e.matmul(parena[:, sb * 512:sb * 512 + 512],
                                     kfull[0:64, pr * S + kc * 128:pr * S + (kc + 1) * 128],
                                     qrope[0:64, pr * T + qt * QT:pr * T + (qt + 1) * QT],
                                     start=True, stop=True, tile_position=(0, 0))
                            m = e.matmul(parena[:, sb * 512 + 512:sb * 512 + 1024],
                                         kfull[64:128, pr * S + kc * 128:pr * S + (kc + 1) * 128],
                                         qrope[64:128, pr * T + qt * QT:pr * T + (qt + 1) * QT],
                                         start=True, stop=True, tile_position=(64, 0))
                            inc(m, "pe")
                            mark(("qk", it0, kc), "pe")
                        if kc >= 1:
                            kp = kc - 1
                            pb = (kp % 2) * KB
                            wt(e, ("exp", it0, kp))
                            e.matmul(parena[0:64, pvb * 512:pvb * 512 + 512],
                                     vfull[:, kp * KB + pr * 128:kp * KB + pr * 128 + 64],
                                     pT[:, pb:pb + 512],
                                     start=(kp == 0), stop=(kp == KC - 1),
                                     tile_position=(0, 0))
                            e.matmul(parena[64:128, pvb * 512:pvb * 512 + 512],
                                     vfull[:, kp * KB + pr * 128 + 64:kp * KB + (pr + 1) * 128],
                                     pT[:, pb + 512:pb + 1024],
                                     start=(kp == 0), stop=(kp == KC - 1),
                                     tile_position=(0, 64))
                            e.matmul(parena[0:1, dnb * 512:dnb * 512 + 512], onesc_bf[:],
                                     pT[:, pb:pb + 512],
                                     start=(kp == 0), stop=(kp == KC - 1),
                                     tile_position=(0, 0))
                            m = e.matmul(parena[64:65, dnb * 512:dnb * 512 + 512], onesc_bf[:],
                                         pT[:, pb + 512:pb + 1024],
                                         start=(kp == 0), stop=(kp == KC - 1),
                                         tile_position=(0, 64))
                            inc(m, "pe")
                            mark(("pv", it0, kp), "pe")
                        if (not SKIP_OP and pr == 0 and qt > 0 and kc >= 4
                                and kc % 2 == 0 and (kc - 4) // 2 < DC):
                            emit_outproj(e, qt - 1, (kc - 4) // 2)
                if SKIP_OP or qt != NQC_E - 1:
                    continue
                for dout in range(DC):
                    emit_outproj(e, qt, dout)
            mark("attn_pe", "pe")
        blk.tensor(attn_pe)

        def attn_ac(e):
            for qt in range(NQC_E):
                for pr in range(NPAIR_E):
                    it0 = (qt * NPAIR + pr)
                    for kc in range(KC):
                        sb = 2 * (kc % 2)
                        pb = (kc % 2) * KB
                        wt(e, ("qk", it0, kc))
                        m = e.activation(out=pT[:, pb:pb + KB],
                                         in_=parena[:, sb * 512:sb * 512 + KB],
                                         func=AF.Exp, bias=negC[:, pr:pr + 1], scale=1.0)
                        inc(m, "ac")
                        mark(("exp", it0, kc), "ac")

        blk.scalar(attn_ac)

        def attn_dv(e):
            for qt in range(NQC_E):
                for pr in range(NPAIR_E):
                    it0 = (qt * NPAIR + pr)
                    wt(e, ("pv", it0, KC - 1))
                    if it0 > 0:
                        wt(e, ("bc", it0 - 1))
                    dnb = 5 + 2 * (pr % 2)
                    m = e.tensor_copy(dens[:, :], parena[0:65, dnb * 512:(dnb + 1) * 512])
                    inc(m, "dv")
                    mark(("denc", it0), "dv")
                    wait(e, "dv", val("dv"))
                    m = e.reciprocal(recips[:, :], dens[:, :])
                    inc(m, "dv")
                    mark(("recip", it0), "dv")
                    wt(e, ("bc", it0))
                    if pr == 0 and qt > 0:
                        wt(e, ("opmm", qt - 1, DC - 1))
                    pvb = 4 + 2 * (pr % 2)
                    m = e.tensor_tensor(out=attn_fm[:, pr * QT:(pr + 1) * QT],
                                        in0=parena[:, pvb * 512:(pvb + 1) * 512],
                                        in1=rbs[:, :], op=ALU.mult)
                    inc(m, "dv")
                    mark(("epiTT", it0), "dv")
                if SKIP_OP:
                    continue
                wt(e, ("x2ld", qt))
                for dout in range(DC):
                    wt(e, ("opmm", qt, dout))
                    m = e.tensor_tensor(out=x2stg[:, dout * QT:(dout + 1) * QT],
                                        in0=bank(OPB),
                                        in1=x2stg[:, dout * QT:(dout + 1) * QT],
                                        op=ALU.add)
                    inc(m, "dv")
                    mark(("opev", qt, dout), "dv")
                mark(("x2rdy", qt), "dv")
            mark("attn_dv", "dv")
        blk.vector(attn_dv)

        def attn_sync(e):
            wt(e, "x1spilled")
            wt(e, ("qp", "ropedone"))
            for qt in range(NQC_E):
                if not SKIP_OP:
                    if qt >= 1:
                        wt(e, ("x2st", qt - 1))
                    dgo(e, "x2l", x2stg[:, :],
                        x1sp[:, :].rearrange("p (c t) -> p c t", c=DC)[:, :, qt * QT:(qt + 1) * QT])
                    mark(("x2ld", qt), "x2l")
                for pr in range(NPAIR_E):
                    it0 = qt * NPAIR + pr
                    wt(e, ("recip", it0))
                    if it0 > 0:
                        wt(e, ("epiTT", it0 - 1))
                    dgo(e, "bcd", rbd[0:1, :], recips[0:1, :])
                    dgo(e, "bcd", rbd[1:2, :], recips[64:65, :])
                    dgo(e, "bcd", rbs[0:64, :], rbd[0:1, :].broadcast_to((64, QT)))
                    dgo(e, "bcd", rbs[64:128, :], rbd[1:2, :].broadcast_to((64, QT)))
                    mark(("bc", it0), "bcd")
                if not SKIP_OP:
                    wt(e, ("x2rdy", qt))
                    dgo(e, "x2s",
                        x2sp[:, :].rearrange("p (c t) -> p c t", c=DC)[:, :, qt * QT:(qt + 1) * QT],
                        x2stg[:, :])
                    mark(("x2st", qt), "x2s")
            mark("x2done", "x2s")
        blk.sync(attn_sync)

        if dbg in ("attn1", "attn1b"):
            def dumpa1(e):
                wt(e, ("epiTT", NPAIR_E - 1))
                dgo(e, "dbg", dbg_out[:, 0:DC * QT], attn_fm[:])
                dgo(e, "dbg", dbg_out2[0:65, 0:QT], dens[:])
                e.wait_ge(sems["dbg"], cnt["dbg"])
            blk.gpsimd(dumpa1)
            return

        if dbg == "attn2":
            def dumpa2(e):
                wt(e, ("x2st", 0))
                dgo(e, "dbg", dbg_out[:, 0:DC * QT], x2stg[:])
                e.wait_ge(sems["dbg"], cnt["dbg"])
            blk.gpsimd(dumpa2)
            return

        if dbg == "attn":
            def dumpa(e):
                wt(e, "x2done")
                dgo(e, "dbg", xs[:, :], x2sp[:, :])
                wait(e, "dbg", val("dbg"))
                dgo(e, "dbg", dbg_out[:, 0:DC * T], xs[:])
                wait(e, "dbg", val("dbg"))
            blk.sync(dumpa)
            return nc

        def x2_reload(e):
            wt(e, "x2done")
            dgo(e, "x2l", xs[:, 0:4 * T], x2sp[:, 0:4 * T])
            mark(("x2h", 0), "x2l")
        blk.sync(x2_reload)

        def x2_reload_hi(e):
            wt(e, "x2done")
            dgo(e, "x2m", xs[:, 4 * T:DC * T], x2sp[:, 4 * T:DC * T])
            mark(("x2h", 1), "x2m")
        blk.gpsimd(x2_reload_hi)

        def x2gate(e):
            wt(e, ("x2h", 0))
            wt(e, ("x2h", 1))
            wt(e, "attn_dv")
            inc(e.nop(), "dv")
            mark("x2ok", "dv")
        blk.vector(x2gate)

        emit_ln_stats("ln3", ["attn_dv"],
                      chunk_keys=lambda c: [("x2h", c // 4)])
        emit_ffn(3, gate_keys=["attn_pe", "attn_dv"])

        def out_pe(e):
            wt(e, ("ffn3", "dv"))
            for tt in range(NTT):
                m = None
                for c in range(DC):
                    if c == 0 and tt >= 1:
                        wt(e, ("oev4", tt - 1))
                    if c == 4 and tt >= 1:
                        wt(e, ("oev", tt - 1))
                    m = e.transpose(parena[:, c * 512:c * 512 + 128],
                                    xs_c(c, tt * 128, (tt + 1) * 128), cs_idf[:])
                    if c == 3:
                        inc(m, "pe")
                        mark(("oT4", tt), "pe")
                inc(m, "pe")
                mark(("oT", tt), "pe")
        blk.tensor(out_pe)

        def out_dv(e):
            for tt in range(NTT):
                sg = stg if tt % 2 == 0 else stg2
                if tt >= 2:
                    wt(e, ("ost", tt - 2))
                wt(e, ("oT4", tt))
                m = None
                for c in range(DC):
                    if c == 4:
                        wt(e, ("oT", tt))
                    m = e.tensor_copy(sg[:, c * 128:(c + 1) * 128],
                                      parena[:, c * 512:c * 512 + 128])
                    if c == 3:
                        inc(m, "dv")
                        mark(("oev4", tt), "dv")
                inc(m, "dv")
                mark(("oev", tt), "dv")
        blk.vector(out_dv)

        def out_sync(e, par, fam):
            for tt in range(par, NTT, 2):
                sg = stg if tt % 2 == 0 else stg2
                wt(e, ("oev", tt))
                dgo(e, fam, y_out[tt * 128:(tt + 1) * 128, :], sg[:])
                mark(("ost", tt), fam)
            wait(e, "sty", val("sty"))
            wait(e, "sty2", val("sty2"))
        blk.sync(lambda e: out_sync(e, 0, "sty"))
        blk.gpsimd(lambda e: out_sync(e, 1, "sty2"))

    return None


def _host_tables(core):
    inv = 10000.0 ** (-np.arange(32, dtype=np.float64) / 32.0)
    pos = (core % 2) * T + np.arange(T, dtype=np.float64)
    f = inv[:, None] * pos[None, :]
    cos, sin = np.cos(f), np.sin(f)
    ca64 = np.concatenate([cos, sin], axis=0)
    cb64 = np.concatenate([-sin, cos], axis=0)
    CA = np.concatenate([ca64, ca64], axis=0)
    CB = np.concatenate([cb64, cb64], axis=0)
    caq = np.concatenate([CA, CB], axis=1) * 0.125
    cak = np.concatenate([CA, CB], axis=1)
    return caq.astype(np.float32), cak.astype(np.float32)


def _perm_mats():
    pa = np.zeros((128, 128), dtype=np.float32)
    pb = np.zeros((128, 128), dtype=np.float32)
    for hp in range(2):
        for i in range(64):
            src = 64 * hp + 2 * (i % 32)
            pa[src, 64 * hp + i] = 1.0
            pb[src + 1, 64 * hp + i] = 1.0
    return np.concatenate([pa, pb], axis=1).astype(ml_dtypes.bfloat16)


def kernel(L0_attn_x, L0_attn_Wq, L0_attn_bq, L0_attn_Wk, L0_attn_bk,
           L0_attn_Wv, L0_attn_bv, L0_attn_Wo, L0_attn_bo,
           L0_ffn_w_gate, L0_ffn_w_up, L0_ffn_w_down, _dbg=None, _trace=False):
    x = np.ascontiguousarray(np.asarray(L0_attn_x, dtype=np.float32))
    Wq = np.asarray(L0_attn_Wq, dtype=np.float32)
    Wk = np.asarray(L0_attn_Wk, dtype=np.float32)
    Wv = np.asarray(L0_attn_Wv, dtype=np.float32)
    Wo = np.asarray(L0_attn_Wo, dtype=np.float32)
    wg = np.asarray(L0_ffn_w_gate, dtype=np.int32)
    wu = np.asarray(L0_ffn_w_up, dtype=np.int32)
    wd = np.asarray(L0_ffn_w_down, dtype=np.int32)

    def attw(Wm):
        return np.ascontiguousarray(
            Wm.reshape(DC, 128, DC, 128).transpose(2, 3, 0, 1).reshape(DC, 128, D))

    wqL, wkL, woL = attw(Wq), attw(Wk), attw(Wo)
    wvL = np.ascontiguousarray(Wv.T.reshape(DC, 128, D))
    wgL = wg.reshape(NH, 128, DC, 128).transpose(0, 3, 2, 1).reshape(NH, 128, D)
    wuL = wu.reshape(NH, 128, DC, 128).transpose(0, 3, 2, 1).reshape(NH, 128, D)
    wguL = np.ascontiguousarray(np.concatenate([wgL, wuL], axis=2))
    wdL = np.ascontiguousarray(wd.reshape(DC, 128, NH, 128).transpose(2, 3, 0, 1).reshape(NH, 128, D))
    biases = np.ascontiguousarray(np.stack([
        np.asarray(L0_attn_bk, dtype=np.float32),
        np.asarray(L0_attn_bv, dtype=np.float32),
        np.asarray(L0_attn_bq, dtype=np.float32),
        np.asarray(L0_attn_bo, dtype=np.float32)]))
    ident = np.eye(128, dtype=np.float32)
    permAB = _perm_mats()
    fconsts = np.zeros((1, 16), dtype=np.float32)
    fconsts[0, 0] = EPS
    fconsts[0, 1] = WS

    nc = build_program(dbg=_dbg)
    in_maps = []
    for core in range(NCORES := 8):
        b, half = core // 2, core % 2
        caq_t, cak_t = _host_tables(core)
        in_maps.append({
            "x_in": np.ascontiguousarray(x[b, half * T:(half + 1) * T, :]),
            "wqL": wqL, "wkL": wkL, "wvL": wvL, "woL": woL, "biases": biases,
            "wguL": wguL, "wdL": wdL,
            "caq": caq_t, "cak": cak_t, "ident": ident, "permAB": permAB,
            "fconsts": fconsts,
        })
    res = run_bass_kernel_spmd(nc, in_maps, core_ids=list(range(8)), trace=_trace)
    global LAST_RES
    LAST_RES = res
    if _dbg is not None:
        return res
    y = np.empty((B, S, D), dtype=np.float32)
    for core in range(8):
        b, half = core // 2, core % 2
        y[b, half * T:(half + 1) * T, :] = res.results[core]["y_out"]
    return y
